# version 10
# baseline (speedup 1.0000x reference)
import math

import numpy as np
import ml_dtypes
import concourse.bass as bass
import concourse.mybir as mybir
from concourse.bass_utils import run_bass_kernel_spmd

F32 = mybir.dt.float32
BF16 = mybir.dt.bfloat16
AF = mybir.ActivationFunctionType
ALU = mybir.AluOpType
AX = mybir.AxisListType

NCORES = 8
S = 8192
T = 1024
NT = 8
H = 2048
NC = 16
NH = 16
HD = 128
E = 8
FF = 4096
EPS = 1e-5
NORM = 1.0 / math.sqrt(HD)
PHASE_LIMIT = 3
DEBUG = False


class Buf:
    __slots__ = ("name", "base", "w", "r", "dsem")
    _uid = [0]

    def __init__(self, name):
        Buf._uid[0] += 1
        self.base = name
        self.name = f"{name}_{Buf._uid[0]}"
        self.w = None
        self.r = []
        self.dsem = None


class KB:

    def __init__(self, nc, stack):
        self.nc = nc
        self.stack = stack
        self.ops = {e: [] for e in ("sp", "pe", "act", "dve", "gp")}
        self.sem = {}
        self.cnt = {}
        self.waited = {e: {} for e in self.ops}
        self.carry = []
        for name in ("sp_c", "pe_c", "act_c", "dve_c", "gp_c", "cc_c", "dmisc"):
            self.getsem(name)

    def getsem(self, name):
        if name not in self.sem:
            self.sem[name] = self.stack.enter_context(self.nc.semaphore(name))
            self.cnt[name] = 0
        return self.sem[name]

    def emit(self, eng, build, reads=(), writes=(), dma=False, cc=False,
             extra_waits=(), dsem=None):
        waits = {}

        def add(tok):
            if tok is None:
                return
            sname, val = tok
            if eng == "pe" and sname == "pe_c":
                return
            if val > waits.get(sname, -1):
                waits[sname] = val

        for b in reads:
            add(b.w)
        for b in writes:
            add(b.w)
            for t in b.r:
                add(t)
        for t in extra_waits:
            add(t)
        wl = []
        for sname, val in waits.items():
            if self.waited[eng].get(sname, -1) >= val:
                continue
            self.waited[eng][sname] = val
            wl.append((sname, val))
        if cc:
            sname, inc = "cc_c", 1
        elif dma:
            if dsem is not None:
                sname = dsem
            elif writes:
                if writes[0].dsem is None:
                    writes[0].dsem = "D" + writes[0].base
                sname = writes[0].dsem
            else:
                sname = "dmisc"
            self.getsem(sname)
            inc = 16
        else:
            sname, inc = {"sp": "sp_c", "pe": "pe_c", "act": "act_c",
                          "dve": "dve_c", "gp": "gp_c"}[eng], 1
        self.cnt[sname] += inc
        tok = (sname, self.cnt[sname])
        self.ops[eng].append((wl, build, (sname, inc)))
        for b in reads:
            b.r.append(tok)
        for b in writes:
            b.w = tok
            b.r = []
        return tok

    def replay(self, block):
        def runner(name):
            def _(eng):
                for wl, build, (sname, inc) in self.ops[name]:
                    for s2, val in wl:
                        eng.wait_ge(self.sem[s2], val)
                    inst = build(eng)
                    if inc:
                        inst.then_inc(self.sem[sname], inc)
            return _
        block.sync(runner("sp"))
        block.tensor(runner("pe"))
        block.scalar(runner("act"))
        block.vector(runner("dve"))
        block.gpsimd(runner("gp"))


def build_program(phase_limit):
    from contextlib import ExitStack
    nc = bass.Bass("TRN2", target_bir_lowering=False, debug=False)

    x_in = nc.dram_tensor("x", [T, H], F32, kind="ExternalInput").ap()
    out = nc.dram_tensor("out", [T, H], F32, kind="ExternalOutput").ap()
    w3d = nc.dram_tensor("w3d", [NH, 128, 6144], F32, kind="ExternalInput").ap()
    wp4 = nc.dram_tensor("wp4", [64, 128, 512], F32, kind="ExternalInput").ap()
    bprojrow = nc.dram_tensor("bprojrow", [1, H], F32, kind="ExternalInput").ap()
    wvT = nc.dram_tensor("wvT", [H, 512], F32, kind="ExternalInput").ap()
    fprojT = nc.dram_tensor("fprojT", [H, H], F32, kind="ExternalInput").ap()
    routerw = nc.dram_tensor("routerw", [H, E], F32, kind="ExternalInput").ap()
    g1c_in = nc.dram_tensor("g1c", [128, NC], F32, kind="ExternalInput").ap()
    b1c_in = nc.dram_tensor("b1c", [128, NC], F32, kind="ExternalInput").ap()
    g2c_in = nc.dram_tensor("g2c", [128, NC], F32, kind="ExternalInput").ap()
    b2c_in = nc.dram_tensor("b2c", [128, NC], F32, kind="ExternalInput").ap()
    ident_in = nc.dram_tensor("ident", [128, 128], BF16, kind="ExternalInput").ap()
    w1d = nc.dram_tensor("w1d", [E * 16, 128, 4096], F32, kind="ExternalInput").ap()
    w2d = nc.dram_tensor("w2d", [E * 4 * 32, 128, 512], F32,
                         kind="ExternalInput").ap()

    pjt_dram = nc.dram_tensor("pjt_dram", [512, H], BF16)
    agki_t = nc.dram_tensor("agki_t", [H, T], BF16)
    agko_t = nc.dram_tensor("agko_t", [2 * NCORES * 1024, T], BF16)
    agvi_t = nc.dram_tensor("agvi_t", [2 * 128, 8 * 1024], BF16)
    agvo_t = nc.dram_tensor("agvo_t", [2 * NCORES * 128, 8 * 1024], BF16)
    agni_t = nc.dram_tensor("agni_t", [2, 8], F32)
    agno_t = nc.dram_tensor("agno_t", [2 * NCORES, 8], F32)

    dbg = {}
    if DEBUG:
        for name, shape in (("d_ncb", [128, NH]), ("d_knq", [1, 32]),
                            ("d_kn8", [16, 16]), ("d_vbuf", [128, 1040]),
                            ("d_ctm", [128, H]), ("d_qfm", [128, T]),
                            ("d_agk", [128, T]), ("d_agv", [128, H]),
                            ("d_p", [128, 1024]), ("d_bpb", [128, H]),
                            ("d_stra", [128, 8192]), ("d_cfm", [128, T])):
            dbg[name] = nc.dram_tensor(name, shape, F32, kind="ExternalOutput").ap()

    stack = ExitStack()
    with stack:
        kb = KB(nc, stack)

        def sb(st, name, shape, dt):
            return st.enter_context(nc.sbuf_tensor(name, shape, dt))

        X = sb(stack, "X", [128, NT * H], F32)
        FMA = sb(stack, "FMA", [128, NC * T], BF16)
        FMB = sb(stack, "FMB", [128, NC * T], BF16)
        IDN = sb(stack, "IDN", [128, 128], BF16)
        ONE1 = sb(stack, "ONE1", [128, 1], BF16)
        ONER = sb(stack, "ONER", [1, 128], BF16)
        G1C = sb(stack, "G1C", [128, NC], F32)
        B1C = sb(stack, "B1C", [128, NC], F32)
        G2C = sb(stack, "G2C", [128, NC], F32)
        B2C = sb(stack, "B2C", [128, NC], F32)
        RTW = sb(stack, "RTW", [128, NC * E], BF16)
        STAT = sb(stack, "STAT", [128, 64], F32)
        GTS = sb(stack, "GTS", [128, NT * E], F32)
        SMALL = sb(stack, "SMALL", [128, 64], F32)
        KNQ = sb(stack, "KNQ", [1, 2 * NH], F32)
        NCB = sb(stack, "NCB", [128, NH], F32)
        KN8 = sb(stack, "KN8", [16, 16], F32)
        KN8B = sb(stack, "KN8B", [16, 16], BF16)
        CEPS = sb(stack, "CEPS", [128, 2], F32)

        PA = stack.enter_context(nc.psum_tensor("PA", [128, 1024], F32))
        PB = stack.enter_context(nc.psum_tensor("PB", [128, 1024], F32))
        PC = stack.enter_context(nc.psum_tensor("PC", [128, 1024], F32))
        PD = stack.enter_context(nc.psum_tensor("PD", [128, 512], F32))
        PT = stack.enter_context(nc.psum_tensor("PT", [128, 1024], BF16))

        bX = [Buf(f"X{t}") for t in range(NT)]
        bFMA = [Buf(f"FMA{c}") for c in range(NC)]
        bFMB = [Buf(f"FMB{c}") for c in range(NC)]
        bCONST = Buf("CONST")
        bIDN = Buf("IDN")
        bGB = Buf("GB")
        bRTW = Buf("RTW")
        bSTAT = Buf("STAT")
        bGTS = Buf("GTS")
        bSMALL = Buf("SMALL")
        bKNQ = Buf("KNQ")
        bNCB = Buf("NCB")
        bKN8 = Buf("KN8")
        bKN8B = Buf("KN8B")
        bPA = [Buf("PA0"), Buf("PA1")]
        bPB = [Buf("PB0"), Buf("PB1")]
        bPC = [Buf("PC0"), Buf("PC1")]
        bPD = Buf("PD")
        bPT = Buf("PT")
        bPJTD = Buf("pjtd")
        bAGK = [Buf(f"agk{h}") for h in range(NH)]
        bAGV = [Buf(f"agv{i}") for i in range(NH * NT)]
        bAGN = [Buf("agn0"), Buf("agn1")]
        bAGKo = [Buf("agko0"), Buf("agko1")]
        bAGVo = [Buf("agvo0"), Buf("agvo1")]
        bAGNo = [Buf("agno0"), Buf("agno1")]

        halves = {"PA": (PA, bPA), "PB": (PB, bPB), "PC": (PC, bPC)}

        def ph(nm, hh):
            t, b = halves[nm]
            return t[:, hh * 512:(hh + 1) * 512], b[hh]

        def fma_u(c0, c1):
            return [bFMA[i] for i in range(c0, c1)]

        def fmb_u(c0, c1):
            return [bFMB[i] for i in range(c0, c1)]

        def scope_barrier(bufs):
            toks = []
            for b in bufs:
                if b.w is not None:
                    toks.append(b.w)
                toks.extend(b.r)
            kb.carry = toks

        def scoped_buf(name):
            b = Buf(name)
            b.r.extend(kb.carry)
            return b

        def merge_into(src_bufs, dst_bufs):
            toks = []
            for b in src_bufs:
                if b.w is not None:
                    toks.append(b.w)
                toks.extend(b.r)
                b.w = None
                b.r = []
            for b in dst_bufs:
                b.r.extend(toks)

        kb.emit("sp", lambda e: e.dma_start(out=IDN[:, :], in_=ident_in),
                writes=[bIDN], dma=True)
        gbufs = [Buf(f"GB{i}") for i in range(4)]
        for i, (dst, srcp) in enumerate(((G1C, g1c_in), (B1C, b1c_in),
                                         (G2C, g2c_in), (B2C, b2c_in))):
            kb.emit("sp", lambda e, dst=dst, srcp=srcp: e.dma_start(
                out=dst[:, :], in_=srcp), writes=[gbufs[i]], dma=True)
        merge_into(gbufs, [bGB])
        kb.emit("dve", lambda e: e.memset(ONE1[:, :], 1.0), writes=[bCONST])
        kb.emit("dve", lambda e: e.memset(ONER[:, :], 1.0), writes=[bCONST])
        kb.emit("dve", lambda e: e.memset(CEPS[:, 0:1], EPS), writes=[bCONST])
        kb.emit("dve", lambda e: e.memset(CEPS[:, 1:2], EPS * EPS), writes=[bCONST])
        for tt in range(NT):
            kb.emit(("sp", "act", "gp")[tt % 3], lambda e, tt=tt: e.dma_start(
                out=X[:, tt * H:(tt + 1) * H],
                in_=x_in[tt * 128:(tt + 1) * 128, :]), writes=[bX[tt]], dma=True)
        for c in range(NC):
            kb.emit("gp", lambda e, c=c: e.dma_start(
                out=RTW[:, c * E:(c + 1) * E],
                in_=routerw[c * 128:(c + 1) * 128, :]), writes=[bRTW], dma=True)

        with ExitStack() as pst:
            PSL = sb(pst, "PSL", [128, 2 * H], BF16)
            PJA = sb(pst, "PJA", [128, H], BF16)
            bPSL = [Buf("psl0"), Buf("psl1")]
            bPJA = Buf("pja")
            for g in range(4):
                for j in range(4):
                    sl = bPSL[j % 2]
                    kb.emit("gp", lambda e, g=g, j=j: e.dma_start(
                        out=PSL[:, (j % 2) * H:(j % 2 + 1) * H],
                        in_=fprojT[(4 * g + j) * 128:(4 * g + j + 1) * 128, :]),
                        writes=[sl], dma=True)
                    if j == 0:
                        kb.emit("dve", lambda e: e.tensor_copy(
                            PJA[:, :], PSL[:, 0:H]), reads=[sl], writes=[bPJA])
                    else:
                        kb.emit("dve", lambda e, j=j: e.tensor_tensor(
                            PJA[:, :], PJA[:, :], PSL[:, (j % 2) * H:(j % 2 + 1) * H],
                            ALU.add), reads=[sl, bPJA], writes=[bPJA])
                kb.emit("sp", lambda e, g=g: e.dma_start(
                    out=pjt_dram.ap()[g * 128:(g + 1) * 128, :], in_=PJA[:, :]),
                    reads=[bPJA], writes=[bPJTD], dma=True)


        def ln_stats(tt, src_ap, src_bufs, scr_ap, scr_bufs, want_s1):
            b0 = tt * 8
            scr_bufs = list(scr_bufs)
            kb.emit("dve", lambda e: e.reduce_sum(
                STAT[:, b0:b0 + 1], src_ap, axis=AX.X),
                reads=src_bufs, writes=[bSTAT])
            kb.emit("dve", lambda e: e.tensor_scalar(
                STAT[:, b0:b0 + 1], STAT[:, b0:b0 + 1], 1.0 / H, None, ALU.mult),
                reads=[bSTAT], writes=[bSTAT])
            kb.emit("dve", lambda e: e.scalar_tensor_tensor(
                scr_ap, src_ap, STAT[:, b0:b0 + 1], src_ap,
                ALU.subtract, ALU.mult, accum_out=STAT[:, b0 + 1:b0 + 2]),
                reads=src_bufs + [bSTAT], writes=[bSTAT] + scr_bufs)
            kb.emit("dve", lambda e: e.tensor_scalar(
                STAT[:, b0 + 1:b0 + 2], STAT[:, b0 + 1:b0 + 2], 1.0 / H, None,
                ALU.mult), reads=[bSTAT], writes=[bSTAT])
            kb.emit("act", lambda e: e.activation(
                STAT[:, b0 + 2:b0 + 3], STAT[:, b0 + 1:b0 + 2], AF.Ln,
                bias=CEPS[:, 0:1], scale=1.0),
                reads=[bSTAT, bCONST], writes=[bSTAT])
            kb.emit("act", lambda e: e.activation(
                STAT[:, b0 + 2:b0 + 3], STAT[:, b0 + 2:b0 + 3], AF.Exp,
                bias=0.0, scale=-0.5), reads=[bSTAT], writes=[bSTAT])
            if want_s1:
                kb.emit("act", lambda e: e.activation(
                    STAT[:, b0 + 3:b0 + 4], STAT[:, b0 + 1:b0 + 2], AF.Ln,
                    bias=CEPS[:, 1:2], scale=1.0 + EPS),
                    reads=[bSTAT, bCONST], writes=[bSTAT])
                kb.emit("act", lambda e: e.activation(
                    STAT[:, b0 + 3:b0 + 4], STAT[:, b0 + 3:b0 + 4], AF.Exp,
                    bias=0.0, scale=-0.5), reads=[bSTAT], writes=[bSTAT])

        def tile_to_fm(src_ap, src_bufs, dst_sb, dst_bufs, tt, gcol, bcol):
            for grp in range(2):
                for k in range(8):
                    c = grp * 8 + k
                    kb.emit("pe", lambda e, c=c, k=k: e.transpose(
                        PT[:, k * 128:(k + 1) * 128],
                        src_ap[:, c * 128:(c + 1) * 128], IDN[:, :]),
                        reads=list(src_bufs) + [bIDN], writes=[bPT])
                for k in range(8):
                    c = grp * 8 + k
                    dcol = c * T + tt * 128
                    if gcol is not None:
                        kb.emit("dve", lambda e, c=c, k=k, dcol=dcol: e.tensor_scalar(
                            dst_sb[:, dcol:dcol + 128], PT[:, k * 128:(k + 1) * 128],
                            gcol[:, c:c + 1], bcol[:, c:c + 1], ALU.mult, ALU.add),
                            reads=[bPT, bGB], writes=[dst_bufs[c]])
                    else:
                        kb.emit("dve", lambda e, k=k, dcol=dcol: e.tensor_copy(
                            dst_sb[:, dcol:dcol + 128], PT[:, k * 128:(k + 1) * 128]),
                            reads=[bPT], writes=[dst_bufs[c]])

        def ffn_call(call, first=False):
            with ExitStack() as fst:
                SCRH = sb(fst, f"SCRH{call}", [128, H], BF16)
                SCRB = sb(fst, f"SCRB{call}", [128, H], BF16)
                FMB2 = sb(fst, f"FMB2_{call}", [128, NC * T], BF16)
                wst = ExitStack()
                WPJ = sb(wst, f"WPJ{call}", [128, 8192], BF16)
                bWPJ = scoped_buf("WPJ")
                bSCRH = scoped_buf("SCRH")
                bSCRB = scoped_buf("SCRB")
                bFMB2 = [scoped_buf(f"FMB2_{u}") for u in range(NC)]
                vb = [Buf(f"V{j}") for j in range(4)]
                scope_bufs = [bWPJ, bSCRH, bSCRB] + bFMB2

                for c in range(NC):
                    kb.emit("gp", lambda e, c=c: e.dma_start(
                        out=WPJ[:, c * 512:(c + 1) * 512],
                        in_=wvT[c * 128:(c + 1) * 128, :]),
                        writes=[bWPJ], dma=True)

                for tt in range(NT):
                    SCX, bSCX = ((SCRB, bSCRB), (SCRH, bSCRH))[tt % 2]
                    ln_stats(tt, X[:, tt * H:(tt + 1) * H], [bX[tt]],
                             SCX[:, 0:H], [bSCX], True)
                    b0 = tt * 8
                    kb.emit("dve", lambda e, tt=tt, b0=b0, SCX=SCX: e.tensor_scalar(
                        SCX[:, 0:H], X[:, tt * H:(tt + 1) * H],
                        STAT[:, b0:b0 + 1], STAT[:, b0 + 3:b0 + 4],
                        ALU.subtract, ALU.mult),
                        reads=[bX[tt], bSTAT], writes=[bSCX])
                    tile_to_fm(SCX[:, 0:H], [bSCX], FMA, bFMA, tt, G1C, B1C)

                merge_into(fmb_u(0, 4), vb)
                rot = 0
                for j in range(4):
                    for tch in range(2):
                        nm = ["PA", "PB", "PC"][rot % 3]
                        hh = (rot // 3) % 2
                        rot += 1
                        pap, pbuf = ph(nm, hh)

                        def vg(e, j=j, tch=tch, pap=pap):
                            inst = None
                            for c in range(NC):
                                inst = e.matmul(
                                    pap,
                                    WPJ[:, c * 512 + j * 128: c * 512 + (j + 1) * 128],
                                    FMA[:, c * T + tch * 512: c * T + (tch + 1) * 512],
                                    start=(c == 0), stop=(c == NC - 1))
                            return inst
                        kb.emit("pe", vg, reads=[bWPJ] + fma_u(0, NC), writes=[pbuf])
                        kb.emit("dve", lambda e, j=j, tch=tch, pap=pap: e.tensor_copy(
                            FMB[:, j * T + tch * 512: j * T + (tch + 1) * 512], pap),
                            reads=[pbuf], writes=[vb[j]])

                for g in range(4):
                    kb.emit("sp", lambda e, g=g: e.dma_start(
                        out=WPJ[:, g * H:(g + 1) * H],
                        in_=pjt_dram.ap()[g * 128:(g + 1) * 128, :]),
                        reads=[bPJTD], writes=[bWPJ], dma=True)

                for tt in range(NT):
                    b0 = tt * 8
                    accs = []
                    for och in range(4):
                        nm = "PA" if och < 2 else "PB"
                        pap, pbuf = ph(nm, och % 2)

                        def pg(e, tt=tt, och=och, pap=pap):
                            inst = None
                            for g in range(4):
                                inst = e.matmul(
                                    pap,
                                    FMB[:, g * T + tt * 128: g * T + (tt + 1) * 128],
                                    WPJ[:, g * H + och * 512: g * H + (och + 1) * 512],
                                    start=(g == 0), stop=(g == 3))
                            return inst
                        kb.emit("pe", pg, reads=[bWPJ] + vb, writes=[pbuf])
                        accs.append((och, pap, pbuf))
                    kb.emit("dve", lambda e, tt=tt, b0=b0: e.tensor_scalar(
                        SCRH[:, 0:H], X[:, tt * H:(tt + 1) * H],
                        STAT[:, b0:b0 + 1], STAT[:, b0 + 2:b0 + 3],
                        ALU.subtract, ALU.mult),
                        reads=[bX[tt], bSTAT], writes=[bSCRH])
                    for och, pap, pbuf in accs:
                        kb.emit("dve", lambda e, och=och, pap=pap: e.tensor_tensor(
                            SCRH[:, och * 512:(och + 1) * 512],
                            SCRH[:, och * 512:(och + 1) * 512], pap, ALU.add),
                            reads=[pbuf, bSCRH], writes=[bSCRH])
                    kb.emit("dve", lambda e, tt=tt: e.scalar_tensor_tensor(
                        X[:, tt * H:(tt + 1) * H], SCRH[:, 0:H], 0.5,
                        X[:, tt * H:(tt + 1) * H], ALU.mult, ALU.add),
                        reads=[bSCRH, bX[tt]], writes=[bX[tt]])
                    ln_stats(tt, SCRH[:, 0:H], [bSCRH], SCRB[:, 0:H], [bSCRB], False)
                    kb.emit("dve", lambda e, tt=tt, b0=b0: e.tensor_scalar(
                        SCRB[:, 0:H], SCRH[:, 0:H],
                        STAT[:, b0:b0 + 1], STAT[:, b0 + 2:b0 + 3],
                        ALU.subtract, ALU.mult), reads=[bSCRH, bSTAT], writes=[bSCRB])
                    tile_to_fm(SCRB[:, 0:H], [bSCRB], FMA, bFMA, tt, G2C, B2C)

                merge_into(vb, fmb_u(0, 4))
                for tt in range(NT):
                    def rg(e, tt=tt):
                        inst = None
                        for c in range(NC):
                            inst = e.matmul(
                                PD[:, 0:8],
                                FMA[:, c * T + tt * 128: c * T + (tt + 1) * 128],
                                RTW[:, c * E:(c + 1) * E],
                                start=(c == 0), stop=(c == NC - 1))
                        return inst
                    kb.emit("pe", rg, reads=[bRTW] + fma_u(0, NC), writes=[bPD])
                    kb.emit("dve", lambda e: e.tensor_copy(
                        SMALL[:, 0:8], PD[:, 0:8]), reads=[bPD], writes=[bSMALL])
                    kb.emit("dve", lambda e: e.reduce_max(
                        SMALL[:, 8:9], SMALL[:, 0:8], axis=AX.X, negate=True),
                        reads=[bSMALL], writes=[bSMALL])
                    kb.emit("act", lambda e: e.activation(
                        SMALL[:, 24:32], SMALL[:, 0:8], AF.Exp,
                        bias=SMALL[:, 8:9], scale=1.0),
                        reads=[bSMALL], writes=[bSMALL])
                    kb.emit("dve", lambda e: e.max(
                        out=SMALL[:, 40:48], in_=SMALL[:, 24:32]),
                        reads=[bSMALL], writes=[bSMALL])
                    kb.emit("dve", lambda e: e.tensor_tensor(
                        SMALL[:, 9:10], SMALL[:, 40:41], SMALL[:, 41:42], ALU.add),
                        reads=[bSMALL], writes=[bSMALL])
                    kb.emit("dve", lambda e: e.reciprocal(
                        SMALL[:, 10:11], SMALL[:, 9:10]),
                        reads=[bSMALL], writes=[bSMALL])
                    kb.emit("dve", lambda e: e.tensor_scalar(
                        SMALL[:, 10:11], SMALL[:, 10:11], 0.5, None, ALU.mult),
                        reads=[bSMALL], writes=[bSMALL])
                    kb.emit("dve", lambda e: e.tensor_scalar(
                        SMALL[:, 16:24], SMALL[:, 24:32], SMALL[:, 41:42], None,
                        ALU.is_ge), reads=[bSMALL], writes=[bSMALL])
                    kb.emit("dve", lambda e, tt=tt: e.scalar_tensor_tensor(
                        GTS[:, tt * E:(tt + 1) * E], SMALL[:, 24:32],
                        SMALL[:, 10:11], SMALL[:, 16:24], ALU.mult, ALU.mult),
                        reads=[bSMALL], writes=[bGTS])

                wst.close()
                scope_barrier([bWPJ])
                STRM = sb(fst, f"STRM{call}", [128, 3 * 4096], BF16)
                W2STG = sb(fst, f"W2STG{call}", [128, 3 * 512], F32)
                W2S = sb(fst, f"W2S{call}", [128, 3 * 512], BF16)
                bSTRM = [scoped_buf(f"STRM{s}") for s in range(3)]
                bW2STG = [scoped_buf(f"W2STG{s}") for s in range(3)]
                bW2S = [scoped_buf(f"W2S{s}") for s in range(3)]
                scope_bufs.extend(bSTRM + bW2STG + bW2S)
                PTF = PT.bitcast(F32)

                def fmb32(u):
                    if u < NC:
                        return FMB[:, u * 1024:(u + 1) * 1024], bFMB[u]
                    return FMB2[:, (u - NC) * 1024:(u - NC + 1) * 1024], bFMB2[u - NC]

                def zbank(q):
                    if q < 6:
                        t = (PA, PB, PC)[q // 2]
                        return t[:, (q % 2) * 512:(q % 2 + 1) * 512]
                    return PD[:, 0:512] if q == 6 else PTF[:, 0:512]
                zbuf = [bPA[0], bPA[1], bPB[0], bPB[1], bPC[0], bPC[1], bPD, bPT]

                def z1_dma(e_, fblk):
                    slot = fblk % 3
                    kb.emit("gp", lambda e, e_=e_, fblk=fblk, slot=slot: e.dma_start(
                        out=STRM[:, slot * 4096:(slot + 1) * 4096],
                        in_=w1d[e_ * 16 + fblk, :, :]),
                        writes=[bSTRM[slot]], dma=True)

                def z1_unit(fblk, j):
                    slot = fblk % 3
                    u = fblk * 2 + j
                    rot_t, rot_b = ((PA, bPA), (PB, bPB), (PC, bPC))[u % 3]

                    def g(e, slot=slot, j=j, rot_t=rot_t):
                        inst = None
                        for c in range(NC):
                            for tch in range(2):
                                inst = e.matmul(
                                    rot_t[:, tch * 512:(tch + 1) * 512],
                                    STRM[:, slot * 4096 + c * 256 + j * 128:
                                         slot * 4096 + c * 256 + (j + 1) * 128],
                                    FMA[:, c * T + tch * 512:
                                        c * T + (tch + 1) * 512],
                                    start=(c == 0), stop=(c == NC - 1))
                        return inst
                    kb.emit("pe", g, reads=[bSTRM[slot]] + fma_u(0, NC),
                            writes=[rot_b[0], rot_b[1]])
                    fap, fbuf = fmb32(u)
                    kb.emit("act", lambda e, rot_t=rot_t, fap=fap: e.activation(
                        fap, rot_t[:, 0:1024], AF.Gelu, bias=0.0, scale=1.0),
                        reads=[rot_b[0], rot_b[1]], writes=[fbuf])

                def z2_pass(e_, och):
                    for f in range(32):
                        slot = f % 3
                        row = (e_ * 4 + och) * 32 + f
                        kb.emit("sp", lambda e, slot=slot, row=row: e.dma_start(
                            out=W2STG[:, slot * 512:(slot + 1) * 512],
                            in_=w2d[row, :, :]),
                            writes=[bW2STG[slot]], dma=True)
                        kb.emit("dve", lambda e, slot=slot: e.tensor_copy(
                            W2S[:, slot * 512:(slot + 1) * 512],
                            W2STG[:, slot * 512:(slot + 1) * 512]),
                            reads=[bW2STG[slot]], writes=[bW2S[slot]])
                        fap, fbuf = fmb32(f)
                        if f == 0:
                            for q in range(8):
                                kb.emit("pe", lambda e, q=q, slot=slot, fap=fap:
                                        e.matmul(
                                    zbank(q), fap[:, q * 128:(q + 1) * 128],
                                    W2S[:, slot * 512:(slot + 1) * 512],
                                    start=True, stop=False),
                                    reads=[bW2S[slot], fbuf], writes=[zbuf[q]])
                        else:
                            def g(e, f=f, slot=slot, fap=fap):
                                inst = None
                                for q in range(8):
                                    inst = e.matmul(
                                        zbank(q), fap[:, q * 128:(q + 1) * 128],
                                        W2S[:, slot * 512:(slot + 1) * 512],
                                        start=False, stop=(f == 31))
                                return inst
                            kb.emit("pe", g, reads=[bW2S[slot], fbuf], writes=zbuf)
                    for q in range(8):
                        xs = q * H + och * 512
                        kb.emit("dve", lambda e, q=q, xs=xs, e_=e_:
                                e.scalar_tensor_tensor(
                            X[:, xs:xs + 512], zbank(q),
                            GTS[:, q * E + e_: q * E + e_ + 1],
                            X[:, xs:xs + 512], ALU.mult, ALU.add),
                            reads=[zbuf[q], bGTS, bX[q]], writes=[bX[q]])

                for k in range(3):
                    z1_dma(0, k)
                for e_ in range(E):
                    for fblk in range(16):
                        z1_unit(fblk, 0)
                        z1_unit(fblk, 1)
                        if fblk <= 12:
                            z1_dma(e_, fblk + 3)
                    if e_ < E - 1:
                        for k in range(3):
                            z1_dma(e_ + 1, k)
                    for och in range(4):
                        z2_pass(e_, och)
                scope_barrier(scope_bufs)

        def attn_call():
            with ExitStack() as ast:
                QFM = sb(ast, "QFM", [128, NH * T], BF16)
                KSTG = sb(ast, "KSTG", [128, T], BF16)
                VSTG = sb(ast, "VSTG", [128, T], BF16)
                VTT = sb(ast, "VTT", [128, T], BF16)
                SQS = sb(ast, "SQS", [128, T], BF16)
                KBUF = sb(ast, "KBUF", [128, 2 * T], BF16)
                VBUF = sb(ast, "VBUF", [128, 2 * 1040], BF16)
                BPROW = sb(ast, "BPROW", [1, H], BF16)
                BPB = sb(ast, "BPB", [128, H], BF16)
                ASCR = sb(ast, "ASCR", [128, 512], BF16)
                STRA = sb(ast, "STRA", [128, 8192], BF16)
                bQFM = [scoped_buf(f"QFM{c}") for c in range(NC)]
                bKSTG = scoped_buf("KSTG")
                bVSTG = scoped_buf("VSTG")
                bVTT = scoped_buf("VTT")
                bSQS = scoped_buf("SQS")
                bKBUF = [scoped_buf("KB0"), scoped_buf("KB1")]
                bVBUF = [scoped_buf("VB0"), scoped_buf("VB1")]
                bBPROW = scoped_buf("BPROW")
                bBPB = scoped_buf("BPB")
                bASCR = scoped_buf("ASCR")
                bSTRA = Buf("STRA")
                a_scope = (bQFM + [bKSTG, bVSTG, bVTT, bSQS, bBPROW, bBPB,
                                   bASCR] + bKBUF + bVBUF)

                BPROWF = QFM[0:1, 0:2 * H].bitcast(F32)
                bqf_stage = [bQFM[i] for i in range(4)]
                kb.emit("sp", lambda e: e.dma_start(out=BPROWF, in_=bprojrow),
                        writes=bqf_stage, dma=True)
                kb.emit("dve", lambda e: e.tensor_copy(BPROW[:, :], BPROWF),
                        reads=bqf_stage, writes=[bBPROW])
                for j in range(4):
                    kb.emit("pe", lambda e, j=j: e.matmul(
                        PD[:, 0:512], ONER[:, :], BPROW[:, j * 512:(j + 1) * 512],
                        start=True, stop=True),
                        reads=[bBPROW, bCONST], writes=[bPD])
                    kb.emit("dve", lambda e, j=j: e.tensor_copy(
                        BPB[:, j * 512:(j + 1) * 512], PD[:, 0:512]),
                        reads=[bPD], writes=[bBPB])

                for tt in range(NT):
                    S1, bS1, S2, bS2 = ((SQS, bSQS, VTT, bVTT),
                                        (KSTG, bKSTG, VSTG, bVSTG))[tt % 2]
                    ln_stats(tt, X[:, tt * H:(tt + 1) * H], [bX[tt]],
                             FMB[:, 14 * 1024:16 * 1024], [bFMB[14], bFMB[15]],
                             False)
                    b0 = tt * 8
                    kb.emit("dve", lambda e, tt=tt, b0=b0, S1=S1: e.tensor_scalar(
                        S1[:, 0:T], X[:, tt * H: tt * H + T],
                        STAT[:, b0:b0 + 1], STAT[:, b0 + 2:b0 + 3],
                        ALU.subtract, ALU.mult), reads=[bX[tt], bSTAT], writes=[bS1])
                    kb.emit("dve", lambda e, tt=tt, b0=b0, S2=S2: e.tensor_scalar(
                        S2[:, 0:T], X[:, tt * H + T: (tt + 1) * H],
                        STAT[:, b0:b0 + 1], STAT[:, b0 + 2:b0 + 3],
                        ALU.subtract, ALU.mult), reads=[bX[tt], bSTAT], writes=[bS2])
                    for grp in range(2):
                        src = S1 if grp == 0 else S2
                        sbuf = bS1 if grp == 0 else bS2
                        for k in range(8):
                            c = grp * 8 + k
                            kb.emit("pe", lambda e, src=src, k=k: e.transpose(
                                PT[:, k * 128:(k + 1) * 128],
                                src[:, k * 128:(k + 1) * 128], IDN[:, :]),
                                reads=[sbuf, bIDN], writes=[bPT])
                        for k in range(8):
                            c = grp * 8 + k
                            dcol = c * T + tt * 128
                            kb.emit("dve", lambda e, k=k, dcol=dcol: e.tensor_copy(
                                FMA[:, dcol:dcol + 128], PT[:, k * 128:(k + 1) * 128]),
                                reads=[bPT], writes=[bFMA[c]])

                bQSL = [[Buf(f"QSL{s}_{c}") for c in range(2)] for s in range(2)]
                merge_into(fmb_u(0, 12), [b for s in bQSL for b in s])
                for h in range(NH):
                    sl = h % 2
                    slab_us = list(bQSL[sl])
                    for half in range(2):
                        kb.emit("gp", lambda e, h=h, half=half, sl=sl: e.dma_start(
                            out=FMB[:, sl * 6144 + half * 3072:
                                 sl * 6144 + (half + 1) * 3072],
                            in_=w3d[h, :, half * 3072:(half + 1) * 3072]),
                            writes=[bQSL[sl][half]], dma=True)
                    accs = []
                    for j in range(3):
                        for tch in range(2):
                            nm = ["PA", "PB", "PC"][j]
                            pap, pbuf = ph(nm, tch)

                            def qg(e, j=j, tch=tch, sl=sl, pap=pap):
                                inst = None
                                for c in range(NC):
                                    inst = e.matmul(
                                        pap,
                                        FMB[:, sl * 6144 + c * 384 + j * 128:
                                            sl * 6144 + c * 384 + (j + 1) * 128],
                                        FMA[:, c * T + tch * 512:
                                            c * T + (tch + 1) * 512],
                                        start=(c == 0), stop=(c == NC - 1))
                                return inst
                            kb.emit("pe", qg, reads=slab_us + fma_u(0, NC),
                                    writes=[pbuf])
                            accs.append((j, tch, pap, pbuf))
                    for (j, tch, pap, pbuf) in accs:
                        if j == 0:
                            kb.emit("dve", lambda e, h=h, tch=tch, pap=pap:
                                    e.tensor_scalar(
                                QFM[:, h * T + tch * 512: h * T + (tch + 1) * 512],
                                pap, NORM, None, ALU.mult),
                                reads=[pbuf], writes=[bQFM[h]])
                        elif j == 1:
                            kb.emit("dve", lambda e, tch=tch, pap=pap: e.tensor_copy(
                                KSTG[:, tch * 512:(tch + 1) * 512], pap),
                                reads=[pbuf], writes=[bKSTG])
                        else:
                            kb.emit("dve", lambda e, tch=tch, pap=pap: e.tensor_copy(
                                VSTG[:, tch * 512:(tch + 1) * 512], pap),
                                reads=[pbuf], writes=[bVSTG])
                    kb.emit("sp", lambda e, h=h: e.dma_start(
                        out=agki_t.ap()[h * 128:(h + 1) * 128, :], in_=KSTG[:, :]),
                        reads=[bKSTG], writes=[bAGK[h]], dma=True, dsem="agkst")

                    def norm_max(src_ap, src_buf, dstcol):
                        kb.emit("dve", lambda e: e.tensor_mul(
                            SQS[:, :], src_ap, src_ap),
                            reads=[src_buf], writes=[bSQS])
                        kb.emit("pe", lambda e: e.matmul(
                            PD[0:1, 0:512], ONE1[:, :], SQS[:, 0:512],
                            start=True, stop=True),
                            reads=[bSQS, bCONST], writes=[bPD])
                        kb.emit("pe", lambda e: e.matmul(
                            PC[0:1, 0:512], ONE1[:, :], SQS[:, 512:1024],
                            start=True, stop=True),
                            reads=[bSQS, bCONST], writes=[bPC[0]])
                        kb.emit("dve", lambda e: e.reduce_max(
                            SMALL[0:1, 0:1], PD[0:1, 0:512], axis=AX.X),
                            reads=[bPD], writes=[bSMALL])
                        kb.emit("dve", lambda e: e.reduce_max(
                            SMALL[0:1, 1:2], PC[0:1, 0:512], axis=AX.X),
                            reads=[bPC[0]], writes=[bSMALL])
                        kb.emit("dve", lambda e, dstcol=dstcol: e.tensor_tensor(
                            KNQ[0:1, dstcol:dstcol + 1], SMALL[0:1, 0:1],
                            SMALL[0:1, 1:2], ALU.max),
                            reads=[bSMALL], writes=[bKNQ])
                    norm_max(KSTG[:, :], bKSTG, h)
                    norm_max(QFM[:, h * T:(h + 1) * T], bQFM[h], NH + h)

                    for tt in range(NT):
                        kb.emit("pe", lambda e, tt=tt: e.transpose(
                            PT[:, tt * 128:(tt + 1) * 128],
                            VSTG[:, tt * 128:(tt + 1) * 128], IDN[:, :]),
                            reads=[bVSTG, bIDN], writes=[bPT])
                    kb.emit("dve", lambda e: e.tensor_copy(VTT[:, :], PT[:, 0:1024]),
                            reads=[bPT], writes=[bVTT])
                    for tt in range(NT):
                        kb.emit("act", lambda e, h=h, tt=tt: e.dma_start(
                            out=agvi_t.ap()[(h // 8) * 128:(h // 8) * 128 + 128,
                                            (h % 8) * 1024 + tt * 128:
                                            (h % 8) * 1024 + (tt + 1) * 128],
                            in_=VTT[:, tt * 128:(tt + 1) * 128]),
                            reads=[bVTT], writes=[bAGV[h * NT + tt]], dma=True,
                            dsem="agvst")

                    if h % 8 == 7:
                        hh = h // 8
                        kb.emit("sp", lambda e, hh=hh: e.dma_start(
                            out=agni_t.ap()[hh:hh + 1, :],
                            in_=KNQ[0:1, hh * 8:hh * 8 + 8]),
                            reads=[bKNQ], writes=[bAGN[hh]], dma=True)
                        kb.emit("gp", lambda e, hh=hh: e.collective_compute(
                            "AllGather", ALU.bypass,
                            replica_groups=[list(range(NCORES))],
                            ins=[agni_t.ap()[hh:hh + 1, :].opt()],
                            outs=[agno_t.ap()[hh * NCORES:(hh + 1) * NCORES,
                                              :].opt()]),
                            reads=[bAGN[hh]], writes=[bAGNo[hh]], cc=True)
                        kb.emit("gp", lambda e, hh=hh: e.collective_compute(
                            "AllGather", ALU.bypass,
                            replica_groups=[list(range(NCORES))],
                            ins=[agki_t.ap()[hh * 1024:(hh + 1) * 1024, :].opt()],
                            outs=[agko_t.ap()[hh * NCORES * 1024:
                                              (hh + 1) * NCORES * 1024, :].opt()]),
                            reads=[bAGK[h2] for h2 in range(hh * 8, hh * 8 + 8)],
                            writes=[bAGKo[hh]], cc=True)
                        kb.emit("gp", lambda e, hh=hh: e.collective_compute(
                            "AllGather", ALU.bypass,
                            replica_groups=[list(range(NCORES))],
                            ins=[agvi_t.ap()[hh * 128:(hh + 1) * 128, :].opt()],
                            outs=[agvo_t.ap()[hh * NCORES * 128:
                                              (hh + 1) * NCORES * 128, :].opt()]),
                            reads=[bAGV[h2 * NT + tt2]
                                   for h2 in range(hh * 8, hh * 8 + 8)
                                   for tt2 in range(NT)],
                            writes=[bAGVo[hh]], cc=True)

                merge_into([b for s in bQSL for b in s], fmb_u(0, 12))

                def ncb_half(hh):
                    c0 = hh * 8
                    kb.emit("sp", lambda e, hh=hh: e.dma_start(
                        out=KN8[0:8, 0:8],
                        in_=agno_t.ap()[hh * NCORES:(hh + 1) * NCORES, :]),
                        reads=[bAGNo[hh]], writes=[bKN8], dma=True)
                    kb.emit("dve", lambda e: e.tensor_copy(
                        KN8B[0:8, 0:8], KN8[0:8, 0:8]),
                        reads=[bKN8], writes=[bKN8B])
                    kb.emit("pe", lambda e: e.transpose(
                        PT[0:8, 0:8], KN8B[0:8, 0:8], IDN[0:8, 0:8]),
                        reads=[bKN8B, bIDN], writes=[bPT])
                    kb.emit("dve", lambda e: e.tensor_copy(
                        KN8B[0:8, 8:16], PT[0:8, 0:8]), reads=[bPT], writes=[bKN8B])
                    kb.emit("dve", lambda e: e.reduce_max(
                        SMALL[0:8, 0:1], KN8B[0:8, 8:16], axis=AX.X),
                        reads=[bKN8B], writes=[bSMALL])
                    kb.emit("dve", lambda e, c0=c0: e.tensor_copy(
                        KN8B[0:1, 0:8], KNQ[0:1, NH + c0:NH + c0 + 8]),
                        reads=[bKNQ], writes=[bKN8B])
                    kb.emit("pe", lambda e: e.transpose(
                        PT[0:8, 32:33], KN8B[0:1, 0:8], IDN[0:1, 0:1]),
                        reads=[bKN8B, bIDN], writes=[bPT])
                    kb.emit("dve", lambda e: e.tensor_copy(
                        SMALL[0:8, 1:2], PT[0:8, 32:33]),
                        reads=[bPT], writes=[bSMALL])
                    kb.emit("dve", lambda e: e.tensor_tensor(
                        SMALL[0:8, 2:3], SMALL[0:8, 0:1], SMALL[0:8, 1:2],
                        ALU.mult), reads=[bSMALL], writes=[bSMALL])
                    kb.emit("act", lambda e: e.activation(
                        SMALL[0:8, 3:4], SMALL[0:8, 2:3], AF.Sqrt,
                        bias=0.0, scale=1.0), reads=[bSMALL], writes=[bSMALL])
                    kb.emit("dve", lambda e: e.tensor_scalar(
                        SMALL[0:8, 4:5], SMALL[0:8, 3:4], -1.0, None, ALU.mult),
                        reads=[bSMALL], writes=[bSMALL])
                    kb.emit("dve", lambda e: e.tensor_copy(
                        KN8B[0:8, 0:1], SMALL[0:8, 4:5]),
                        reads=[bSMALL], writes=[bKN8B])
                    kb.emit("pe", lambda e: e.transpose(
                        PT[0:1, 0:8], KN8B[0:8, 0:1], IDN[0:8, 0:8]),
                        reads=[bKN8B, bIDN], writes=[bPT])
                    kb.emit("dve", lambda e: e.tensor_copy(
                        KN8B[0:1, 0:8], PT[0:1, 0:8]), reads=[bPT], writes=[bKN8B])
                    kb.emit("pe", lambda e: e.matmul(
                        PD[:, 0:8], ONER[:, :], KN8B[0:1, 0:8],
                        start=True, stop=True),
                        reads=[bKN8B, bCONST], writes=[bPD])
                    kb.emit("dve", lambda e, c0=c0: e.tensor_copy(
                        NCB[:, c0:c0 + 8], PD[:, 0:8]),
                        reads=[bPD], writes=[bNCB])
                if DEBUG:
                    kb.emit("sp", lambda e: e.dma_start(
                        out=dbg["d_ncb"], in_=NCB[:, :]), reads=[bNCB], dma=True)
                    kb.emit("sp", lambda e: e.dma_start(
                        out=dbg["d_knq"], in_=KNQ[:, :]), reads=[bKNQ], dma=True)
                    kb.emit("sp", lambda e: e.dma_start(
                        out=dbg["d_kn8"], in_=KN8[:, :]), reads=[bKN8], dma=True)
                    kb.emit("gp", lambda e: e.dma_start(
                        out=dbg["d_qfm"], in_=QFM[:, 0:T]),
                        reads=[bQFM[0]], dma=True)
                    kb.emit("gp", lambda e: e.dma_start(
                        out=dbg["d_agk"], in_=agko_t.ap()[0:128, :]),
                        reads=[bAGKo], dma=True)
                    kb.emit("gp", lambda e: e.dma_start(
                        out=dbg["d_agv"], in_=agvo_t.ap()[0:128, :]),
                        reads=[bAGVo], dma=True)

                kb.emit("dve", lambda e: e.memset(VBUF[:, :], 1.0),
                        writes=[bVBUF[0], bVBUF[1]])

                def ctx_ap(q):
                    if q < 3:
                        return PC[:, q * 129:(q + 1) * 129]
                    if q < 6:
                        return PC[:, 512 + (q - 3) * 129: 512 + (q - 2) * 129]
                    return PD[:, (q - 6) * 129:(q - 5) * 129]
                ctx_wb = [bPC[0], bPC[1], bPD]

                pending_ctx = [None]

                def flush_ctx():
                    if pending_ctx[0] is not None:
                        cgf, rds = pending_ctx[0]
                        pending_ctx[0] = None
                        kb.emit("pe", cgf, reads=rds, writes=ctx_wb)

                for h in range(NH):
                    if h % 8 == 0:
                        ncb_half(h // 8)
                    kbase = (h // 8) * NCORES * 1024 + (h % 8) * 128
                    vbase = (h // 8) * NCORES * 128
                    vcol = (h % 8) * 1024
                    for blk in range(8):
                        pb = blk % 2
                        kb.emit("sp", lambda e, blk=blk, pb=pb, kbase=kbase:
                                e.dma_start(
                            out=KBUF[:, pb * T:(pb + 1) * T],
                            in_=agko_t.ap()[kbase + blk * 1024:
                                            kbase + blk * 1024 + 128, :]),
                            reads=[bAGKo[h // 8]], writes=[bKBUF[pb]], dma=True)
                        kb.emit("sp", lambda e, blk=blk, pb=pb, vbase=vbase,
                                vcol=vcol: e.dma_start(
                            out=VBUF[:, pb * 1040: pb * 1040 + 1040]
                            .rearrange("p (kt r) -> p kt r", kt=8)[:, :, 0:128],
                            in_=agvo_t.ap()[vbase + blk * 128:
                                            vbase + (blk + 1) * 128,
                                            vcol:vcol + 1024]
                            .rearrange("p (kt d) -> p kt d", kt=8)),
                            reads=[bAGVo[h // 8]], writes=[bVBUF[pb]], dma=True)
                        for kt in range(8):
                            nm = "PA" if kt % 2 == 0 else "PB"
                            pten, pbufs = halves[nm]

                            def stg(e, h=h, pb=pb, kt=kt, pten=pten):
                                inst = None
                                for qch in range(2):
                                    inst = e.matmul(
                                        pten[:, qch * 512:(qch + 1) * 512],
                                        KBUF[:, pb * T + kt * 128:
                                             pb * T + (kt + 1) * 128],
                                        QFM[:, h * T + qch * 512:
                                            h * T + (qch + 1) * 512],
                                        start=True, stop=True)
                                return inst
                            kb.emit("pe", stg, reads=[bKBUF[pb], bQFM[h]],
                                    writes=[pbufs[0], pbufs[1]])
                            pcol = (pb * 8 + kt) * 1024
                            kb.emit("act", lambda e, h=h, pcol=pcol, pten=pten:
                                    e.activation(
                                FMB[:, pcol:pcol + 1024], pten[:, 0:1024],
                                AF.Exp, bias=NCB[:, h:h + 1], scale=1.0),
                                reads=[pbufs[0], pbufs[1], bNCB],
                                writes=[bFMB[pcol // 1024]])
                            flush_ctx()
                            first = (blk == 0 and kt == 0)
                            last = (blk == 7 and kt == 7)

                            def cg(e, pb=pb, kt=kt, pcol=pcol, first=first, last=last):
                                inst = None
                                for q in range(8):
                                    st = first and (q in (0, 3, 6))
                                    inst = e.matmul(
                                        ctx_ap(q),
                                        FMB[:, pcol + q * 128: pcol + (q + 1) * 128],
                                        VBUF[:, pb * 1040 + kt * 130:
                                             pb * 1040 + kt * 130 + 129],
                                        start=st, stop=last)
                                return inst
                            pending_ctx[0] = (cg, [bFMB[pcol // 1024], bVBUF[pb]])
                    flush_ctx()
                    for q in range(8):
                        kb.emit("dve", lambda e, q=q: e.reciprocal(
                            SMALL[:, 32 + q:33 + q], ctx_ap(q)[:, 128:129]),
                            reads=ctx_wb, writes=[bSMALL])
                    for q in range(8):
                        kb.emit("dve", lambda e, h=h, q=q: e.tensor_scalar(
                            FMA[:, q * H + h * 128: q * H + (h + 1) * 128],
                            ctx_ap(q)[:, 0:128], SMALL[:, 32 + q:33 + q], None,
                            ALU.mult),
                            reads=ctx_wb + [bSMALL],
                            writes=[bFMA[(q * H + h * 128) // T]])

                if DEBUG:
                    kb.emit("gp", lambda e: e.dma_start(
                        out=dbg["d_ctm"], in_=FMA[:, 0:H]),
                        reads=[bFMA[0], bFMA[1]], dma=True)
                    kb.emit("gp", lambda e: e.dma_start(
                        out=dbg["d_p"], in_=FMB[:, 0:1024]),
                        reads=[bFMB[0]], dma=True)
                for tt in range(NT):
                    for grp in range(2):
                        for k in range(8):
                            c = grp * 8 + k
                            kb.emit("pe", lambda e, tt=tt, c=c, k=k: e.transpose(
                                PT[:, k * 128:(k + 1) * 128],
                                FMA[:, tt * H + c * 128: tt * H + (c + 1) * 128],
                                IDN[:, :]),
                                reads=[bFMA[(tt * H + c * 128) // T], bIDN],
                                writes=[bPT])
                        for k in range(8):
                            c = grp * 8 + k
                            dcol = c * T + tt * 128
                            kb.emit("dve", lambda e, k=k, dcol=dcol: e.tensor_copy(
                                QFM[:, dcol:dcol + 128], PT[:, k * 128:(k + 1) * 128]),
                                reads=[bPT], writes=[bQFM[dcol // T]])
                bSTRAc = [Buf(f"STRA{c}") for c in range(NC)]
                for och in range(4):
                    for c in range(NC):
                        kb.emit("gp", lambda e, och=och, c=c: e.dma_start(
                            out=STRA[:, c * 512:(c + 1) * 512],
                            in_=wp4[och * 16 + c, :, :]),
                            writes=[bSTRAc[c]], dma=True, dsem="straS")
                    if DEBUG and och == 0:
                        kb.emit("gp", lambda e: e.dma_start(
                            out=dbg["d_stra"], in_=STRA[:, :]),
                            reads=list(bSTRAc), dma=True)
                        kb.emit("gp", lambda e: e.dma_start(
                            out=dbg["d_bpb"], in_=BPB[:, :]),
                            reads=[bBPB], dma=True)
                        kb.emit("gp", lambda e: e.dma_start(
                            out=dbg["d_cfm"], in_=QFM[:, 0:T]),
                            reads=[bQFM[0]], dma=True)
                    for ttblk in range(2):
                        accs = [ph("PA", 0), ph("PA", 1), ph("PB", 0), ph("PB", 1)]
                        for c in range(NC):
                            def pj(e, och=och, ttblk=ttblk, c=c, accs=accs):
                                inst = None
                                for q in range(4):
                                    tt = ttblk * 4 + q
                                    inst = e.matmul(
                                        accs[q][0],
                                        QFM[:, c * T + tt * 128: c * T + (tt + 1) * 128],
                                        STRA[:, c * 512:(c + 1) * 512],
                                        start=(c == 0), stop=(c == NC - 1))
                                return inst
                            kb.emit("pe", pj, reads=[bSTRAc[c], bQFM[c]],
                                    writes=[a[1] for a in accs])
                        for q in range(4):
                            tt = ttblk * 4 + q
                            kb.emit("dve", lambda e, q=q, och=och, accs=accs:
                                    e.tensor_tensor(
                                ASCR[:, 0:512], accs[q][0],
                                BPB[:, och * 512:(och + 1) * 512], ALU.add),
                                reads=[accs[q][1], bBPB], writes=[bASCR])
                            kb.emit("dve", lambda e, tt=tt, och=och: e.tensor_tensor(
                                X[:, tt * H + och * 512: tt * H + (och + 1) * 512],
                                X[:, tt * H + och * 512: tt * H + (och + 1) * 512],
                                ASCR[:, 0:512], ALU.add),
                                reads=[bASCR, bX[tt]], writes=[bX[tt]])
                scope_barrier(a_scope + bSTRAc
                              + [b for s in bQSL for b in s])

        ffn_call(0, first=True)
        if phase_limit >= 2:
            attn_call()
        if phase_limit >= 3:
            ffn_call(1)

        for tt in range(NT):
            kb.emit("sp", lambda e, tt=tt: e.dma_start(
                out=out[tt * 128:(tt + 1) * 128, :],
                in_=X[:, tt * H:(tt + 1) * H]), reads=[bX[tt]], dma=True,
                dsem="outs")
        kb.emit("sp", lambda e: e.nop(),
                extra_waits=[("outs", kb.cnt["outs"])])

        block = stack.enter_context(nc.Block())
        kb.replay(block)

    return nc


def kernel(**inputs):
    _ = inputs.get("L0_attn_key")
    _ = inputs.get("L0_attn_value")
    x = np.asarray(inputs["L0_attn_query"], dtype=np.float32).reshape(S, H)
    idx = np.concatenate([np.arange(g * 768 + 640, g * 768 + 768) for g in range(4)])
    qkvw = np.asarray(inputs["L0_ffn_qkv_weight"], dtype=np.float32)
    wvT = np.ascontiguousarray(qkvw[idx].T)
    fprojT = np.ascontiguousarray(
        np.asarray(inputs["L0_ffn_proj_weight"], dtype=np.float32).T)
    wqkvT = np.asarray(inputs["L0_attn_Wqkv"], dtype=np.float32).T
    w3d = np.ascontiguousarray(
        wqkvT.reshape(16, 128, 16, 384).transpose(2, 1, 0, 3)).reshape(
            16, 128, 6144)
    wprojT = np.asarray(inputs["L0_attn_Wproj"], dtype=np.float32).T
    wp4 = np.ascontiguousarray(
        wprojT.reshape(16, 128, 4, 512).transpose(2, 0, 1, 3)).reshape(
            64, 128, 512)
    bprojrow = np.ascontiguousarray(
        np.asarray(inputs["L0_attn_bproj"], dtype=np.float32).reshape(1, H))
    routerw = np.ascontiguousarray(
        np.asarray(inputs["L0_ffn_router_weight"], dtype=np.float32))
    g1c = np.ascontiguousarray(
        np.asarray(inputs["L0_ffn_ln1_weight"], dtype=np.float32).reshape(NC, 128).T)
    b1c = np.ascontiguousarray(
        np.asarray(inputs["L0_ffn_ln1_bias"], dtype=np.float32).reshape(NC, 128).T)
    g2c = np.ascontiguousarray(
        np.asarray(inputs["L0_ffn_ln2_weight"], dtype=np.float32).reshape(NC, 128).T)
    b2c = np.ascontiguousarray(
        np.asarray(inputs["L0_ffn_ln2_bias"], dtype=np.float32).reshape(NC, 128).T)
    ident = np.eye(128, dtype=ml_dtypes.bfloat16)
    moe_w1 = np.asarray(inputs["L0_ffn_moe_w1"], dtype=np.float32)
    moe_w2 = np.asarray(inputs["L0_ffn_moe_w2"], dtype=np.float32)
    w1d = np.ascontiguousarray(
        moe_w1.reshape(8, 16, 128, 16, 256).transpose(0, 3, 2, 1, 4)).reshape(
            128, 128, 4096)
    w2d = np.ascontiguousarray(
        moe_w2.reshape(8, 32, 128, 4, 512).transpose(0, 3, 1, 2, 4)).reshape(
            1024, 128, 512)

    nc = build_program(PHASE_LIMIT)
    in_maps = []
    for i in range(NCORES):
        in_maps.append({
            "x": np.ascontiguousarray(x[i * T:(i + 1) * T]),
            "w3d": w3d, "wp4": wp4, "bprojrow": bprojrow,
            "wvT": wvT, "fprojT": fprojT, "routerw": routerw,
            "g1c": g1c, "b1c": b1c, "g2c": g2c, "b2c": b2c,
            "ident": ident, "w1d": w1d, "w2d": w2d,
        })
    res = run_bass_kernel_spmd(nc, in_maps, core_ids=list(range(NCORES)))
    shards = [np.asarray(res.results[i]["out"], dtype=np.float32)
              for i in range(NCORES)]
    return np.concatenate(shards, axis=0).reshape(1, S, H)


# version 11
# speedup vs baseline: 1.0031x; 1.0031x over previous
import math

import numpy as np
import ml_dtypes
import concourse.bass as bass
import concourse.mybir as mybir
from concourse.bass_utils import run_bass_kernel_spmd

F32 = mybir.dt.float32
BF16 = mybir.dt.bfloat16
AF = mybir.ActivationFunctionType
ALU = mybir.AluOpType
AX = mybir.AxisListType

NCORES = 8
S = 8192
T = 1024
NT = 8
H = 2048
NC = 16
NH = 16
HD = 128
E = 8
FF = 4096
EPS = 1e-5
NORM = 1.0 / math.sqrt(HD)
PHASE_LIMIT = 3
DEBUG = False


class Buf:
    __slots__ = ("name", "base", "w", "r", "dsem")
    _uid = [0]

    def __init__(self, name):
        Buf._uid[0] += 1
        self.base = name
        self.name = f"{name}_{Buf._uid[0]}"
        self.w = None
        self.r = []
        self.dsem = None


class KB:

    def __init__(self, nc, stack):
        self.nc = nc
        self.stack = stack
        self.ops = {e: [] for e in ("sp", "pe", "act", "dve", "gp")}
        self.sem = {}
        self.cnt = {}
        self.waited = {e: {} for e in self.ops}
        self.carry = []
        for name in ("sp_c", "pe_c", "act_c", "dve_c", "gp_c", "cc_c", "dmisc"):
            self.getsem(name)

    def getsem(self, name):
        if name not in self.sem:
            self.sem[name] = self.stack.enter_context(self.nc.semaphore(name))
            self.cnt[name] = 0
        return self.sem[name]

    def emit(self, eng, build, reads=(), writes=(), dma=False, cc=False,
             extra_waits=(), dsem=None):
        waits = {}

        def add(tok):
            if tok is None:
                return
            sname, val = tok
            if eng == "pe" and sname == "pe_c":
                return
            if val > waits.get(sname, -1):
                waits[sname] = val

        for b in reads:
            add(b.w)
        for b in writes:
            add(b.w)
            for t in b.r:
                add(t)
        for t in extra_waits:
            add(t)
        wl = []
        for sname, val in waits.items():
            if self.waited[eng].get(sname, -1) >= val:
                continue
            self.waited[eng][sname] = val
            wl.append((sname, val))
        if cc:
            sname, inc = "cc_c", 1
        elif dma:
            if dsem is not None:
                sname = dsem
            elif writes:
                if writes[0].dsem is None:
                    writes[0].dsem = "D" + writes[0].base
                sname = writes[0].dsem
            else:
                sname = "dmisc"
            self.getsem(sname)
            inc = 16
        else:
            sname, inc = {"sp": "sp_c", "pe": "pe_c", "act": "act_c",
                          "dve": "dve_c", "gp": "gp_c"}[eng], 1
        self.cnt[sname] += inc
        tok = (sname, self.cnt[sname])
        self.ops[eng].append((wl, build, (sname, inc)))
        for b in reads:
            b.r.append(tok)
        for b in writes:
            b.w = tok
            b.r = []
        return tok

    def replay(self, block):
        def runner(name):
            def _(eng):
                for wl, build, (sname, inc) in self.ops[name]:
                    for s2, val in wl:
                        eng.wait_ge(self.sem[s2], val)
                    inst = build(eng)
                    if inc:
                        inst.then_inc(self.sem[sname], inc)
            return _
        block.sync(runner("sp"))
        block.tensor(runner("pe"))
        block.scalar(runner("act"))
        block.vector(runner("dve"))
        block.gpsimd(runner("gp"))


def build_program(phase_limit):
    from contextlib import ExitStack
    nc = bass.Bass("TRN2", target_bir_lowering=False, debug=False)

    x_in = nc.dram_tensor("x", [T, H], F32, kind="ExternalInput").ap()
    out = nc.dram_tensor("out", [T, H], F32, kind="ExternalOutput").ap()
    w3d = nc.dram_tensor("w3d", [NH, 128, 6144], F32, kind="ExternalInput").ap()
    wp4 = nc.dram_tensor("wp4", [64, 128, 512], F32, kind="ExternalInput").ap()
    bprojrow = nc.dram_tensor("bprojrow", [1, H], F32, kind="ExternalInput").ap()
    wvT = nc.dram_tensor("wvT", [H, 512], F32, kind="ExternalInput").ap()
    fprojT = nc.dram_tensor("fprojT", [H, H], F32, kind="ExternalInput").ap()
    routerw = nc.dram_tensor("routerw", [H, E], F32, kind="ExternalInput").ap()
    g1c_in = nc.dram_tensor("g1c", [128, NC], F32, kind="ExternalInput").ap()
    b1c_in = nc.dram_tensor("b1c", [128, NC], F32, kind="ExternalInput").ap()
    g2c_in = nc.dram_tensor("g2c", [128, NC], F32, kind="ExternalInput").ap()
    b2c_in = nc.dram_tensor("b2c", [128, NC], F32, kind="ExternalInput").ap()
    ident_in = nc.dram_tensor("ident", [128, 128], BF16, kind="ExternalInput").ap()
    w1d = nc.dram_tensor("w1d", [E * 16, 128, 4096], F32, kind="ExternalInput").ap()
    w2d = nc.dram_tensor("w2d", [E * 4 * 32, 128, 512], F32,
                         kind="ExternalInput").ap()

    pjt_dram = nc.dram_tensor("pjt_dram", [512, H], BF16)
    agki_t = nc.dram_tensor("agki_t", [H, T], BF16)
    agko_t = nc.dram_tensor("agko_t", [2 * NCORES * 1024, T], BF16)
    agvi_t = nc.dram_tensor("agvi_t", [2 * T, 1024], BF16)
    agvo_t = nc.dram_tensor("agvo_t", [2 * NCORES * T, 1024], BF16)
    agni_t = nc.dram_tensor("agni_t", [2, 8], F32)
    agno_t = nc.dram_tensor("agno_t", [2 * NCORES, 8], F32)

    dbg = {}
    if DEBUG:
        for name, shape in (("d_ncb", [128, NH]), ("d_knq", [1, 32]),
                            ("d_kn8", [16, 16]), ("d_vbuf", [128, 1040]),
                            ("d_ctm", [128, H]), ("d_qfm", [128, T]),
                            ("d_agk", [128, T]), ("d_agv", [128, H]),
                            ("d_p", [128, 1024]), ("d_bpb", [128, H]),
                            ("d_stra", [128, 8192]), ("d_cfm", [128, T])):
            dbg[name] = nc.dram_tensor(name, shape, F32, kind="ExternalOutput").ap()

    stack = ExitStack()
    with stack:
        kb = KB(nc, stack)

        def sb(st, name, shape, dt):
            return st.enter_context(nc.sbuf_tensor(name, shape, dt))

        X = sb(stack, "X", [128, NT * H], F32)
        FMA = sb(stack, "FMA", [128, NC * T], BF16)
        FMB = sb(stack, "FMB", [128, NC * T], BF16)
        IDN = sb(stack, "IDN", [128, 128], BF16)
        ONE1 = sb(stack, "ONE1", [128, 1], BF16)
        ONER = sb(stack, "ONER", [1, 128], BF16)
        G1C = sb(stack, "G1C", [128, NC], F32)
        B1C = sb(stack, "B1C", [128, NC], F32)
        G2C = sb(stack, "G2C", [128, NC], F32)
        B2C = sb(stack, "B2C", [128, NC], F32)
        RTW = sb(stack, "RTW", [128, NC * E], BF16)
        STAT = sb(stack, "STAT", [128, 64], F32)
        GTS = sb(stack, "GTS", [128, NT * E], F32)
        SMALL = sb(stack, "SMALL", [128, 64], F32)
        KNQ = sb(stack, "KNQ", [1, 2 * NH], F32)
        NCB = sb(stack, "NCB", [128, NH], F32)
        KN8 = sb(stack, "KN8", [16, 16], F32)
        KN8B = sb(stack, "KN8B", [16, 16], BF16)
        CEPS = sb(stack, "CEPS", [128, 2], F32)

        PA = stack.enter_context(nc.psum_tensor("PA", [128, 1024], F32))
        PB = stack.enter_context(nc.psum_tensor("PB", [128, 1024], F32))
        PC = stack.enter_context(nc.psum_tensor("PC", [128, 1024], F32))
        PD = stack.enter_context(nc.psum_tensor("PD", [128, 512], F32))
        PT = stack.enter_context(nc.psum_tensor("PT", [128, 1024], BF16))

        bX = [Buf(f"X{t}") for t in range(NT)]
        bFMA = [Buf(f"FMA{c}") for c in range(NC)]
        bFMB = [Buf(f"FMB{c}") for c in range(NC)]
        bCONST = Buf("CONST")
        bIDN = Buf("IDN")
        bGB = Buf("GB")
        bRTW = Buf("RTW")
        bSTAT = Buf("STAT")
        bGTS = Buf("GTS")
        bSMALL = Buf("SMALL")
        bKNQ = Buf("KNQ")
        bNCB = Buf("NCB")
        bKN8 = Buf("KN8")
        bKN8B = Buf("KN8B")
        bPA = [Buf("PA0"), Buf("PA1")]
        bPB = [Buf("PB0"), Buf("PB1")]
        bPC = [Buf("PC0"), Buf("PC1")]
        bPD = Buf("PD")
        bPT = Buf("PT")
        bPJTD = Buf("pjtd")
        bAGK = [Buf(f"agk{h}") for h in range(NH)]
        bAGV = [Buf(f"agv{i}") for i in range(NH * NT)]
        bAGN = [Buf("agn0"), Buf("agn1")]
        bAGKo = [Buf("agko0"), Buf("agko1")]
        bAGVo = [Buf("agvo0"), Buf("agvo1")]
        bAGNo = [Buf("agno0"), Buf("agno1")]

        halves = {"PA": (PA, bPA), "PB": (PB, bPB), "PC": (PC, bPC)}

        def ph(nm, hh):
            t, b = halves[nm]
            return t[:, hh * 512:(hh + 1) * 512], b[hh]

        def fma_u(c0, c1):
            return [bFMA[i] for i in range(c0, c1)]

        def fmb_u(c0, c1):
            return [bFMB[i] for i in range(c0, c1)]

        def scope_barrier(bufs):
            toks = []
            for b in bufs:
                if b.w is not None:
                    toks.append(b.w)
                toks.extend(b.r)
            kb.carry = toks

        def scoped_buf(name):
            b = Buf(name)
            b.r.extend(kb.carry)
            return b

        def merge_into(src_bufs, dst_bufs):
            toks = []
            for b in src_bufs:
                if b.w is not None:
                    toks.append(b.w)
                toks.extend(b.r)
                b.w = None
                b.r = []
            for b in dst_bufs:
                b.r.extend(toks)

        kb.emit("sp", lambda e: e.dma_start(out=IDN[:, :], in_=ident_in),
                writes=[bIDN], dma=True)
        for dst, srcp in ((G1C, g1c_in), (B1C, b1c_in),
                          (G2C, g2c_in), (B2C, b2c_in)):
            kb.emit("sp", lambda e, dst=dst, srcp=srcp: e.dma_start(
                out=dst[:, :], in_=srcp), writes=[bGB], dma=True)
        kb.emit("dve", lambda e: e.memset(ONE1[:, :], 1.0), writes=[bCONST])
        kb.emit("dve", lambda e: e.memset(ONER[:, :], 1.0), writes=[bCONST])
        kb.emit("dve", lambda e: e.memset(CEPS[:, 0:1], EPS), writes=[bCONST])
        kb.emit("dve", lambda e: e.memset(CEPS[:, 1:2], EPS * EPS), writes=[bCONST])
        for tt in range(NT):
            kb.emit(("sp", "act", "gp")[tt % 3], lambda e, tt=tt: e.dma_start(
                out=X[:, tt * H:(tt + 1) * H],
                in_=x_in[tt * 128:(tt + 1) * 128, :]), writes=[bX[tt]], dma=True)
        for c in range(NC):
            kb.emit("gp", lambda e, c=c: e.dma_start(
                out=RTW[:, c * E:(c + 1) * E],
                in_=routerw[c * 128:(c + 1) * 128, :]), writes=[bRTW], dma=True)

        with ExitStack() as pst:
            PSL = sb(pst, "PSL", [128, 2 * H], BF16)
            PJA = sb(pst, "PJA", [128, H], BF16)
            bPSL = [Buf("psl0"), Buf("psl1")]
            bPJA = Buf("pja")
            for g in range(4):
                for j in range(4):
                    sl = bPSL[j % 2]
                    kb.emit("gp", lambda e, g=g, j=j: e.dma_start(
                        out=PSL[:, (j % 2) * H:(j % 2 + 1) * H],
                        in_=fprojT[(4 * g + j) * 128:(4 * g + j + 1) * 128, :]),
                        writes=[sl], dma=True)
                    if j == 0:
                        kb.emit("dve", lambda e: e.tensor_copy(
                            PJA[:, :], PSL[:, 0:H]), reads=[sl], writes=[bPJA])
                    else:
                        kb.emit("dve", lambda e, j=j: e.tensor_tensor(
                            PJA[:, :], PJA[:, :], PSL[:, (j % 2) * H:(j % 2 + 1) * H],
                            ALU.add), reads=[sl, bPJA], writes=[bPJA])
                kb.emit("sp", lambda e, g=g: e.dma_start(
                    out=pjt_dram.ap()[g * 128:(g + 1) * 128, :], in_=PJA[:, :]),
                    reads=[bPJA], writes=[bPJTD], dma=True)


        def ln_stats(tt, src_ap, src_bufs, scr_ap, scr_bufs, want_s1):
            b0 = tt * 8
            scr_bufs = list(scr_bufs)
            kb.emit("dve", lambda e: e.reduce_sum(
                STAT[:, b0:b0 + 1], src_ap, axis=AX.X),
                reads=src_bufs, writes=[bSTAT])
            kb.emit("dve", lambda e: e.tensor_scalar(
                STAT[:, b0:b0 + 1], STAT[:, b0:b0 + 1], 1.0 / H, None, ALU.mult),
                reads=[bSTAT], writes=[bSTAT])
            kb.emit("dve", lambda e: e.scalar_tensor_tensor(
                scr_ap, src_ap, STAT[:, b0:b0 + 1], src_ap,
                ALU.subtract, ALU.mult, accum_out=STAT[:, b0 + 1:b0 + 2]),
                reads=src_bufs + [bSTAT], writes=[bSTAT] + scr_bufs)
            kb.emit("dve", lambda e: e.tensor_scalar(
                STAT[:, b0 + 1:b0 + 2], STAT[:, b0 + 1:b0 + 2], 1.0 / H, None,
                ALU.mult), reads=[bSTAT], writes=[bSTAT])
            kb.emit("act", lambda e: e.activation(
                STAT[:, b0 + 2:b0 + 3], STAT[:, b0 + 1:b0 + 2], AF.Ln,
                bias=CEPS[:, 0:1], scale=1.0),
                reads=[bSTAT, bCONST], writes=[bSTAT])
            kb.emit("act", lambda e: e.activation(
                STAT[:, b0 + 2:b0 + 3], STAT[:, b0 + 2:b0 + 3], AF.Exp,
                bias=0.0, scale=-0.5), reads=[bSTAT], writes=[bSTAT])
            if want_s1:
                kb.emit("act", lambda e: e.activation(
                    STAT[:, b0 + 3:b0 + 4], STAT[:, b0 + 1:b0 + 2], AF.Ln,
                    bias=CEPS[:, 1:2], scale=1.0 + EPS),
                    reads=[bSTAT, bCONST], writes=[bSTAT])
                kb.emit("act", lambda e: e.activation(
                    STAT[:, b0 + 3:b0 + 4], STAT[:, b0 + 3:b0 + 4], AF.Exp,
                    bias=0.0, scale=-0.5), reads=[bSTAT], writes=[bSTAT])

        def tile_to_fm(src_ap, src_bufs, dst_sb, dst_bufs, tt, gcol, bcol):
            for grp in range(2):
                for k in range(8):
                    c = grp * 8 + k
                    kb.emit("pe", lambda e, c=c, k=k: e.transpose(
                        PT[:, k * 128:(k + 1) * 128],
                        src_ap[:, c * 128:(c + 1) * 128], IDN[:, :]),
                        reads=list(src_bufs) + [bIDN], writes=[bPT])
                for k in range(8):
                    c = grp * 8 + k
                    dcol = c * T + tt * 128
                    if gcol is not None:
                        kb.emit("dve", lambda e, c=c, k=k, dcol=dcol: e.tensor_scalar(
                            dst_sb[:, dcol:dcol + 128], PT[:, k * 128:(k + 1) * 128],
                            gcol[:, c:c + 1], bcol[:, c:c + 1], ALU.mult, ALU.add),
                            reads=[bPT, bGB], writes=[dst_bufs[c]])
                    else:
                        kb.emit("dve", lambda e, k=k, dcol=dcol: e.tensor_copy(
                            dst_sb[:, dcol:dcol + 128], PT[:, k * 128:(k + 1) * 128]),
                            reads=[bPT], writes=[dst_bufs[c]])

        def ffn_call(call, first=False):
            with ExitStack() as fst:
                SCRH = sb(fst, f"SCRH{call}", [128, H], BF16)
                SCRB = sb(fst, f"SCRB{call}", [128, H], BF16)
                FMB2 = sb(fst, f"FMB2_{call}", [128, NC * T], BF16)
                wst = ExitStack()
                WPJ = sb(wst, f"WPJ{call}", [128, 8192], BF16)
                bWPJ = scoped_buf("WPJ")
                bSCRH = scoped_buf("SCRH")
                bSCRB = scoped_buf("SCRB")
                bFMB2 = [scoped_buf(f"FMB2_{u}") for u in range(NC)]
                vb = [Buf(f"V{j}") for j in range(4)]
                scope_bufs = [bWPJ, bSCRH, bSCRB] + bFMB2

                for c in range(NC):
                    kb.emit("gp", lambda e, c=c: e.dma_start(
                        out=WPJ[:, c * 512:(c + 1) * 512],
                        in_=wvT[c * 128:(c + 1) * 128, :]),
                        writes=[bWPJ], dma=True)

                for tt in range(NT):
                    ln_stats(tt, X[:, tt * H:(tt + 1) * H], [bX[tt]],
                             SCRB[:, 0:H], [bSCRB], True)
                    b0 = tt * 8
                    kb.emit("dve", lambda e, tt=tt, b0=b0: e.tensor_scalar(
                        SCRB[:, 0:H], X[:, tt * H:(tt + 1) * H],
                        STAT[:, b0:b0 + 1], STAT[:, b0 + 3:b0 + 4],
                        ALU.subtract, ALU.mult),
                        reads=[bX[tt], bSTAT], writes=[bSCRB])
                    tile_to_fm(SCRB[:, 0:H], [bSCRB], FMA, bFMA, tt, G1C, B1C)

                merge_into(fmb_u(0, 4), vb)
                rot = 0
                for j in range(4):
                    for tch in range(2):
                        nm = ["PA", "PB", "PC"][rot % 3]
                        hh = (rot // 3) % 2
                        rot += 1
                        pap, pbuf = ph(nm, hh)

                        def vg(e, j=j, tch=tch, pap=pap):
                            inst = None
                            for c in range(NC):
                                inst = e.matmul(
                                    pap,
                                    WPJ[:, c * 512 + j * 128: c * 512 + (j + 1) * 128],
                                    FMA[:, c * T + tch * 512: c * T + (tch + 1) * 512],
                                    start=(c == 0), stop=(c == NC - 1))
                            return inst
                        kb.emit("pe", vg, reads=[bWPJ] + fma_u(0, NC), writes=[pbuf])
                        kb.emit("dve", lambda e, j=j, tch=tch, pap=pap: e.tensor_copy(
                            FMB[:, j * T + tch * 512: j * T + (tch + 1) * 512], pap),
                            reads=[pbuf], writes=[vb[j]])

                for g in range(4):
                    kb.emit("sp", lambda e, g=g: e.dma_start(
                        out=WPJ[:, g * H:(g + 1) * H],
                        in_=pjt_dram.ap()[g * 128:(g + 1) * 128, :]),
                        reads=[bPJTD], writes=[bWPJ], dma=True)

                for tt in range(NT):
                    b0 = tt * 8
                    accs = []
                    for och in range(4):
                        nm = "PA" if och < 2 else "PB"
                        pap, pbuf = ph(nm, och % 2)

                        def pg(e, tt=tt, och=och, pap=pap):
                            inst = None
                            for g in range(4):
                                inst = e.matmul(
                                    pap,
                                    FMB[:, g * T + tt * 128: g * T + (tt + 1) * 128],
                                    WPJ[:, g * H + och * 512: g * H + (och + 1) * 512],
                                    start=(g == 0), stop=(g == 3))
                            return inst
                        kb.emit("pe", pg, reads=[bWPJ] + vb, writes=[pbuf])
                        accs.append((och, pap, pbuf))
                    kb.emit("dve", lambda e, tt=tt, b0=b0: e.tensor_scalar(
                        SCRH[:, 0:H], X[:, tt * H:(tt + 1) * H],
                        STAT[:, b0:b0 + 1], STAT[:, b0 + 2:b0 + 3],
                        ALU.subtract, ALU.mult),
                        reads=[bX[tt], bSTAT], writes=[bSCRH])
                    for och, pap, pbuf in accs:
                        kb.emit("dve", lambda e, och=och, pap=pap: e.tensor_tensor(
                            SCRH[:, och * 512:(och + 1) * 512],
                            SCRH[:, och * 512:(och + 1) * 512], pap, ALU.add),
                            reads=[pbuf, bSCRH], writes=[bSCRH])
                    kb.emit("dve", lambda e, tt=tt: e.scalar_tensor_tensor(
                        X[:, tt * H:(tt + 1) * H], SCRH[:, 0:H], 0.5,
                        X[:, tt * H:(tt + 1) * H], ALU.mult, ALU.add),
                        reads=[bSCRH, bX[tt]], writes=[bX[tt]])
                    ln_stats(tt, SCRH[:, 0:H], [bSCRH], SCRB[:, 0:H], [bSCRB], False)
                    kb.emit("dve", lambda e, tt=tt, b0=b0: e.tensor_scalar(
                        SCRB[:, 0:H], SCRH[:, 0:H],
                        STAT[:, b0:b0 + 1], STAT[:, b0 + 2:b0 + 3],
                        ALU.subtract, ALU.mult), reads=[bSCRH, bSTAT], writes=[bSCRB])
                    tile_to_fm(SCRB[:, 0:H], [bSCRB], FMA, bFMA, tt, G2C, B2C)

                merge_into(vb, fmb_u(0, 4))
                for tt in range(NT):
                    def rg(e, tt=tt):
                        inst = None
                        for c in range(NC):
                            inst = e.matmul(
                                PD[:, 0:8],
                                FMA[:, c * T + tt * 128: c * T + (tt + 1) * 128],
                                RTW[:, c * E:(c + 1) * E],
                                start=(c == 0), stop=(c == NC - 1))
                        return inst
                    kb.emit("pe", rg, reads=[bRTW] + fma_u(0, NC), writes=[bPD])
                    kb.emit("dve", lambda e: e.tensor_copy(
                        SMALL[:, 0:8], PD[:, 0:8]), reads=[bPD], writes=[bSMALL])
                    kb.emit("dve", lambda e: e.reduce_max(
                        SMALL[:, 8:9], SMALL[:, 0:8], axis=AX.X, negate=True),
                        reads=[bSMALL], writes=[bSMALL])
                    kb.emit("act", lambda e: e.activation(
                        SMALL[:, 24:32], SMALL[:, 0:8], AF.Exp,
                        bias=SMALL[:, 8:9], scale=1.0),
                        reads=[bSMALL], writes=[bSMALL])
                    kb.emit("dve", lambda e: e.max(
                        out=SMALL[:, 40:48], in_=SMALL[:, 24:32]),
                        reads=[bSMALL], writes=[bSMALL])
                    kb.emit("dve", lambda e: e.tensor_tensor(
                        SMALL[:, 9:10], SMALL[:, 40:41], SMALL[:, 41:42], ALU.add),
                        reads=[bSMALL], writes=[bSMALL])
                    kb.emit("dve", lambda e: e.reciprocal(
                        SMALL[:, 10:11], SMALL[:, 9:10]),
                        reads=[bSMALL], writes=[bSMALL])
                    kb.emit("dve", lambda e: e.tensor_scalar(
                        SMALL[:, 10:11], SMALL[:, 10:11], 0.5, None, ALU.mult),
                        reads=[bSMALL], writes=[bSMALL])
                    kb.emit("dve", lambda e: e.tensor_scalar(
                        SMALL[:, 16:24], SMALL[:, 24:32], SMALL[:, 41:42], None,
                        ALU.is_ge), reads=[bSMALL], writes=[bSMALL])
                    kb.emit("dve", lambda e, tt=tt: e.scalar_tensor_tensor(
                        GTS[:, tt * E:(tt + 1) * E], SMALL[:, 24:32],
                        SMALL[:, 10:11], SMALL[:, 16:24], ALU.mult, ALU.mult),
                        reads=[bSMALL], writes=[bGTS])

                wst.close()
                scope_barrier([bWPJ])
                STRM = sb(fst, f"STRM{call}", [128, 3 * 4096], BF16)
                W2STG = sb(fst, f"W2STG{call}", [128, 3 * 512], F32)
                W2S = sb(fst, f"W2S{call}", [128, 3 * 512], BF16)
                bSTRM = [scoped_buf(f"STRM{s}") for s in range(3)]
                bW2STG = [scoped_buf(f"W2STG{s}") for s in range(3)]
                bW2S = [scoped_buf(f"W2S{s}") for s in range(3)]
                scope_bufs.extend(bSTRM + bW2STG + bW2S)
                PTF = PT.bitcast(F32)

                def fmb32(u):
                    if u < NC:
                        return FMB[:, u * 1024:(u + 1) * 1024], bFMB[u]
                    return FMB2[:, (u - NC) * 1024:(u - NC + 1) * 1024], bFMB2[u - NC]

                def zbank(q):
                    if q < 6:
                        t = (PA, PB, PC)[q // 2]
                        return t[:, (q % 2) * 512:(q % 2 + 1) * 512]
                    return PD[:, 0:512] if q == 6 else PTF[:, 0:512]
                zbuf = [bPA[0], bPA[1], bPB[0], bPB[1], bPC[0], bPC[1], bPD, bPT]

                def z1_dma(e_, fblk):
                    slot = fblk % 3
                    kb.emit("gp", lambda e, e_=e_, fblk=fblk, slot=slot: e.dma_start(
                        out=STRM[:, slot * 4096:(slot + 1) * 4096],
                        in_=w1d[e_ * 16 + fblk, :, :]),
                        writes=[bSTRM[slot]], dma=True)

                def z1_unit(fblk, j):
                    slot = fblk % 3
                    u = fblk * 2 + j
                    rot_t, rot_b = ((PA, bPA), (PB, bPB), (PC, bPC))[u % 3]

                    def g(e, slot=slot, j=j, rot_t=rot_t):
                        inst = None
                        for c in range(NC):
                            for tch in range(2):
                                inst = e.matmul(
                                    rot_t[:, tch * 512:(tch + 1) * 512],
                                    STRM[:, slot * 4096 + c * 256 + j * 128:
                                         slot * 4096 + c * 256 + (j + 1) * 128],
                                    FMA[:, c * T + tch * 512:
                                        c * T + (tch + 1) * 512],
                                    start=(c == 0), stop=(c == NC - 1))
                        return inst
                    kb.emit("pe", g, reads=[bSTRM[slot]] + fma_u(0, NC),
                            writes=[rot_b[0], rot_b[1]])
                    fap, fbuf = fmb32(u)
                    kb.emit("act", lambda e, rot_t=rot_t, fap=fap: e.activation(
                        fap, rot_t[:, 0:1024], AF.Gelu, bias=0.0, scale=1.0),
                        reads=[rot_b[0], rot_b[1]], writes=[fbuf])

                def z2_pass(e_, och):
                    for f in range(32):
                        slot = f % 3
                        row = (e_ * 4 + och) * 32 + f
                        kb.emit("sp", lambda e, slot=slot, row=row: e.dma_start(
                            out=W2STG[:, slot * 512:(slot + 1) * 512],
                            in_=w2d[row, :, :]),
                            writes=[bW2STG[slot]], dma=True)
                        kb.emit("dve", lambda e, slot=slot: e.tensor_copy(
                            W2S[:, slot * 512:(slot + 1) * 512],
                            W2STG[:, slot * 512:(slot + 1) * 512]),
                            reads=[bW2STG[slot]], writes=[bW2S[slot]])
                        fap, fbuf = fmb32(f)
                        if f == 0:
                            for q in range(8):
                                kb.emit("pe", lambda e, q=q, slot=slot, fap=fap:
                                        e.matmul(
                                    zbank(q), fap[:, q * 128:(q + 1) * 128],
                                    W2S[:, slot * 512:(slot + 1) * 512],
                                    start=True, stop=False),
                                    reads=[bW2S[slot], fbuf], writes=[zbuf[q]])
                        else:
                            def g(e, f=f, slot=slot, fap=fap):
                                inst = None
                                for q in range(8):
                                    inst = e.matmul(
                                        zbank(q), fap[:, q * 128:(q + 1) * 128],
                                        W2S[:, slot * 512:(slot + 1) * 512],
                                        start=False, stop=(f == 31))
                                return inst
                            kb.emit("pe", g, reads=[bW2S[slot], fbuf], writes=zbuf)
                    for q in range(8):
                        xs = q * H + och * 512
                        kb.emit("dve", lambda e, q=q, xs=xs, e_=e_:
                                e.scalar_tensor_tensor(
                            X[:, xs:xs + 512], zbank(q),
                            GTS[:, q * E + e_: q * E + e_ + 1],
                            X[:, xs:xs + 512], ALU.mult, ALU.add),
                            reads=[zbuf[q], bGTS, bX[q]], writes=[bX[q]])

                for k in range(3):
                    z1_dma(0, k)
                for e_ in range(E):
                    for fblk in range(16):
                        z1_unit(fblk, 0)
                        z1_unit(fblk, 1)
                        if fblk <= 12:
                            z1_dma(e_, fblk + 3)
                    if e_ < E - 1:
                        for k in range(3):
                            z1_dma(e_ + 1, k)
                    for och in range(4):
                        z2_pass(e_, och)
                scope_barrier(scope_bufs)

        def attn_call():
            with ExitStack() as ast:
                QFM = sb(ast, "QFM", [128, NH * T], BF16)
                KSTG = sb(ast, "KSTG", [128, T], BF16)
                VSTG = sb(ast, "VSTG", [128, T], BF16)
                VTT = sb(ast, "VTT", [128, T], BF16)
                SQS = sb(ast, "SQS", [128, T], BF16)
                KBUF = sb(ast, "KBUF", [128, 2 * T], BF16)
                VBUF = sb(ast, "VBUF", [128, 2 * 1040], BF16)
                BPROW = sb(ast, "BPROW", [1, H], BF16)
                BPB = sb(ast, "BPB", [128, H], BF16)
                ASCR = sb(ast, "ASCR", [128, 512], BF16)
                STRA = sb(ast, "STRA", [128, 8192], BF16)
                bQFM = [scoped_buf(f"QFM{c}") for c in range(NC)]
                bKSTG = scoped_buf("KSTG")
                bVSTG = scoped_buf("VSTG")
                bVTT = scoped_buf("VTT")
                bSQS = scoped_buf("SQS")
                bKBUF = [scoped_buf("KB0"), scoped_buf("KB1")]
                bVBUF = [scoped_buf("VB0"), scoped_buf("VB1")]
                bBPROW = scoped_buf("BPROW")
                bBPB = scoped_buf("BPB")
                bASCR = scoped_buf("ASCR")
                bSTRA = Buf("STRA")
                a_scope = (bQFM + [bKSTG, bVSTG, bVTT, bSQS, bBPROW, bBPB,
                                   bASCR] + bKBUF + bVBUF)

                BPROWF = QFM[0:1, 0:2 * H].bitcast(F32)
                bqf_stage = [bQFM[i] for i in range(4)]
                kb.emit("sp", lambda e: e.dma_start(out=BPROWF, in_=bprojrow),
                        writes=bqf_stage, dma=True)
                kb.emit("dve", lambda e: e.tensor_copy(BPROW[:, :], BPROWF),
                        reads=bqf_stage, writes=[bBPROW])
                for j in range(4):
                    kb.emit("pe", lambda e, j=j: e.matmul(
                        PD[:, 0:512], ONER[:, :], BPROW[:, j * 512:(j + 1) * 512],
                        start=True, stop=True),
                        reads=[bBPROW, bCONST], writes=[bPD])
                    kb.emit("dve", lambda e, j=j: e.tensor_copy(
                        BPB[:, j * 512:(j + 1) * 512], PD[:, 0:512]),
                        reads=[bPD], writes=[bBPB])

                for tt in range(NT):
                    ln_stats(tt, X[:, tt * H:(tt + 1) * H], [bX[tt]],
                             FMB[:, 14 * 1024:16 * 1024], [bFMB[14], bFMB[15]],
                             False)
                    b0 = tt * 8
                    kb.emit("dve", lambda e, tt=tt, b0=b0: e.tensor_scalar(
                        SQS[:, 0:T], X[:, tt * H: tt * H + T],
                        STAT[:, b0:b0 + 1], STAT[:, b0 + 2:b0 + 3],
                        ALU.subtract, ALU.mult), reads=[bX[tt], bSTAT], writes=[bSQS])
                    kb.emit("dve", lambda e, tt=tt, b0=b0: e.tensor_scalar(
                        VTT[:, 0:T], X[:, tt * H + T: (tt + 1) * H],
                        STAT[:, b0:b0 + 1], STAT[:, b0 + 2:b0 + 3],
                        ALU.subtract, ALU.mult), reads=[bX[tt], bSTAT], writes=[bVTT])
                    for grp in range(2):
                        src = SQS if grp == 0 else VTT
                        sbuf = bSQS if grp == 0 else bVTT
                        for k in range(8):
                            c = grp * 8 + k
                            kb.emit("pe", lambda e, src=src, k=k: e.transpose(
                                PT[:, k * 128:(k + 1) * 128],
                                src[:, k * 128:(k + 1) * 128], IDN[:, :]),
                                reads=[sbuf, bIDN], writes=[bPT])
                        for k in range(8):
                            c = grp * 8 + k
                            dcol = c * T + tt * 128
                            kb.emit("dve", lambda e, k=k, dcol=dcol: e.tensor_copy(
                                FMA[:, dcol:dcol + 128], PT[:, k * 128:(k + 1) * 128]),
                                reads=[bPT], writes=[bFMA[c]])

                bQSL = [[Buf(f"QSL{s}_{c}") for c in range(2)] for s in range(2)]
                merge_into(fmb_u(0, 12), [b for s in bQSL for b in s])
                for h in range(NH):
                    sl = h % 2
                    slab_us = list(bQSL[sl])
                    for half in range(2):
                        kb.emit("gp", lambda e, h=h, half=half, sl=sl: e.dma_start(
                            out=FMB[:, sl * 6144 + half * 3072:
                                 sl * 6144 + (half + 1) * 3072],
                            in_=w3d[h, :, half * 3072:(half + 1) * 3072]),
                            writes=[bQSL[sl][half]], dma=True)
                    accs = []
                    for j in range(3):
                        for tch in range(2):
                            nm = ["PA", "PB", "PC"][j]
                            pap, pbuf = ph(nm, tch)

                            def qg(e, j=j, tch=tch, sl=sl, pap=pap):
                                inst = None
                                for c in range(NC):
                                    inst = e.matmul(
                                        pap,
                                        FMB[:, sl * 6144 + c * 384 + j * 128:
                                            sl * 6144 + c * 384 + (j + 1) * 128],
                                        FMA[:, c * T + tch * 512:
                                            c * T + (tch + 1) * 512],
                                        start=(c == 0), stop=(c == NC - 1))
                                return inst
                            kb.emit("pe", qg, reads=slab_us + fma_u(0, NC),
                                    writes=[pbuf])
                            accs.append((j, tch, pap, pbuf))
                    for (j, tch, pap, pbuf) in accs:
                        if j == 0:
                            kb.emit("dve", lambda e, h=h, tch=tch, pap=pap:
                                    e.tensor_scalar(
                                QFM[:, h * T + tch * 512: h * T + (tch + 1) * 512],
                                pap, NORM, None, ALU.mult),
                                reads=[pbuf], writes=[bQFM[h]])
                        elif j == 1:
                            kb.emit("dve", lambda e, tch=tch, pap=pap: e.tensor_copy(
                                KSTG[:, tch * 512:(tch + 1) * 512], pap),
                                reads=[pbuf], writes=[bKSTG])
                        else:
                            kb.emit("dve", lambda e, tch=tch, pap=pap: e.tensor_copy(
                                VSTG[:, tch * 512:(tch + 1) * 512], pap),
                                reads=[pbuf], writes=[bVSTG])
                    kb.emit("sp", lambda e, h=h: e.dma_start(
                        out=agki_t.ap()[h * 128:(h + 1) * 128, :], in_=KSTG[:, :]),
                        reads=[bKSTG], writes=[bAGK[h]], dma=True, dsem="agkst")

                    def norm_max(src_ap, src_buf, dstcol):
                        kb.emit("dve", lambda e: e.tensor_mul(
                            SQS[:, :], src_ap, src_ap),
                            reads=[src_buf], writes=[bSQS])
                        kb.emit("pe", lambda e: e.matmul(
                            PD[0:1, 0:512], ONE1[:, :], SQS[:, 0:512],
                            start=True, stop=True),
                            reads=[bSQS, bCONST], writes=[bPD])
                        kb.emit("pe", lambda e: e.matmul(
                            PC[0:1, 0:512], ONE1[:, :], SQS[:, 512:1024],
                            start=True, stop=True),
                            reads=[bSQS, bCONST], writes=[bPC[0]])
                        kb.emit("dve", lambda e: e.reduce_max(
                            SMALL[0:1, 0:1], PD[0:1, 0:512], axis=AX.X),
                            reads=[bPD], writes=[bSMALL])
                        kb.emit("dve", lambda e: e.reduce_max(
                            SMALL[0:1, 1:2], PC[0:1, 0:512], axis=AX.X),
                            reads=[bPC[0]], writes=[bSMALL])
                        kb.emit("dve", lambda e, dstcol=dstcol: e.tensor_tensor(
                            KNQ[0:1, dstcol:dstcol + 1], SMALL[0:1, 0:1],
                            SMALL[0:1, 1:2], ALU.max),
                            reads=[bSMALL], writes=[bKNQ])
                    norm_max(KSTG[:, :], bKSTG, h)
                    norm_max(QFM[:, h * T:(h + 1) * T], bQFM[h], NH + h)

                    for tt in range(NT):
                        kb.emit("pe", lambda e, tt=tt: e.transpose(
                            PT[:, tt * 128:(tt + 1) * 128],
                            VSTG[:, tt * 128:(tt + 1) * 128], IDN[:, :]),
                            reads=[bVSTG, bIDN], writes=[bPT])
                    kb.emit("dve", lambda e: e.tensor_copy(VTT[:, :], PT[:, 0:1024]),
                            reads=[bPT], writes=[bVTT])
                    for tt in range(NT):
                        kb.emit("act", lambda e, h=h, tt=tt: e.dma_start(
                            out=agvi_t.ap()[(h // 8) * T + tt * 128:
                                            (h // 8) * T + (tt + 1) * 128,
                                            (h % 8) * 128:(h % 8 + 1) * 128],
                            in_=VTT[:, tt * 128:(tt + 1) * 128]),
                            reads=[bVTT], writes=[bAGV[h * NT + tt]], dma=True,
                            dsem="agvst")

                    if h % 8 == 7:
                        hh = h // 8
                        kb.emit("sp", lambda e, hh=hh: e.dma_start(
                            out=agni_t.ap()[hh:hh + 1, :],
                            in_=KNQ[0:1, hh * 8:hh * 8 + 8]),
                            reads=[bKNQ], writes=[bAGN[hh]], dma=True)
                        kb.emit("gp", lambda e, hh=hh: e.collective_compute(
                            "AllGather", ALU.bypass,
                            replica_groups=[list(range(NCORES))],
                            ins=[agni_t.ap()[hh:hh + 1, :].opt()],
                            outs=[agno_t.ap()[hh * NCORES:(hh + 1) * NCORES,
                                              :].opt()]),
                            reads=[bAGN[hh]], writes=[bAGNo[hh]], cc=True)
                        kb.emit("gp", lambda e, hh=hh: e.collective_compute(
                            "AllGather", ALU.bypass,
                            replica_groups=[list(range(NCORES))],
                            ins=[agki_t.ap()[hh * 1024:(hh + 1) * 1024, :].opt()],
                            outs=[agko_t.ap()[hh * NCORES * 1024:
                                              (hh + 1) * NCORES * 1024, :].opt()]),
                            reads=[bAGK[h2] for h2 in range(hh * 8, hh * 8 + 8)],
                            writes=[bAGKo[hh]], cc=True)
                        kb.emit("gp", lambda e, hh=hh: e.collective_compute(
                            "AllGather", ALU.bypass,
                            replica_groups=[list(range(NCORES))],
                            ins=[agvi_t.ap()[hh * T:(hh + 1) * T, :].opt()],
                            outs=[agvo_t.ap()[hh * NCORES * T:
                                              (hh + 1) * NCORES * T, :].opt()]),
                            reads=[bAGV[h2 * NT + tt2]
                                   for h2 in range(hh * 8, hh * 8 + 8)
                                   for tt2 in range(NT)],
                            writes=[bAGVo[hh]], cc=True)

                merge_into([b for s in bQSL for b in s], fmb_u(0, 12))

                def ncb_half(hh):
                    c0 = hh * 8
                    kb.emit("sp", lambda e, hh=hh: e.dma_start(
                        out=KN8[0:8, 0:8],
                        in_=agno_t.ap()[hh * NCORES:(hh + 1) * NCORES, :]),
                        reads=[bAGNo[hh]], writes=[bKN8], dma=True)
                    kb.emit("dve", lambda e: e.tensor_copy(
                        KN8B[0:8, 0:8], KN8[0:8, 0:8]),
                        reads=[bKN8], writes=[bKN8B])
                    kb.emit("pe", lambda e: e.transpose(
                        PT[0:8, 0:8], KN8B[0:8, 0:8], IDN[0:8, 0:8]),
                        reads=[bKN8B, bIDN], writes=[bPT])
                    kb.emit("dve", lambda e: e.tensor_copy(
                        KN8B[0:8, 8:16], PT[0:8, 0:8]), reads=[bPT], writes=[bKN8B])
                    kb.emit("dve", lambda e: e.reduce_max(
                        SMALL[0:8, 0:1], KN8B[0:8, 8:16], axis=AX.X),
                        reads=[bKN8B], writes=[bSMALL])
                    kb.emit("dve", lambda e, c0=c0: e.tensor_copy(
                        KN8B[0:1, 0:8], KNQ[0:1, NH + c0:NH + c0 + 8]),
                        reads=[bKNQ], writes=[bKN8B])
                    kb.emit("pe", lambda e: e.transpose(
                        PT[0:8, 32:33], KN8B[0:1, 0:8], IDN[0:1, 0:1]),
                        reads=[bKN8B, bIDN], writes=[bPT])
                    kb.emit("dve", lambda e: e.tensor_copy(
                        SMALL[0:8, 1:2], PT[0:8, 32:33]),
                        reads=[bPT], writes=[bSMALL])
                    kb.emit("dve", lambda e: e.tensor_tensor(
                        SMALL[0:8, 2:3], SMALL[0:8, 0:1], SMALL[0:8, 1:2],
                        ALU.mult), reads=[bSMALL], writes=[bSMALL])
                    kb.emit("act", lambda e: e.activation(
                        SMALL[0:8, 3:4], SMALL[0:8, 2:3], AF.Sqrt,
                        bias=0.0, scale=1.0), reads=[bSMALL], writes=[bSMALL])
                    kb.emit("dve", lambda e: e.tensor_scalar(
                        SMALL[0:8, 4:5], SMALL[0:8, 3:4], -1.0, None, ALU.mult),
                        reads=[bSMALL], writes=[bSMALL])
                    kb.emit("dve", lambda e: e.tensor_copy(
                        KN8B[0:8, 0:1], SMALL[0:8, 4:5]),
                        reads=[bSMALL], writes=[bKN8B])
                    kb.emit("pe", lambda e: e.transpose(
                        PT[0:1, 0:8], KN8B[0:8, 0:1], IDN[0:8, 0:8]),
                        reads=[bKN8B, bIDN], writes=[bPT])
                    kb.emit("dve", lambda e: e.tensor_copy(
                        KN8B[0:1, 0:8], PT[0:1, 0:8]), reads=[bPT], writes=[bKN8B])
                    kb.emit("pe", lambda e: e.matmul(
                        PD[:, 0:8], ONER[:, :], KN8B[0:1, 0:8],
                        start=True, stop=True),
                        reads=[bKN8B, bCONST], writes=[bPD])
                    kb.emit("dve", lambda e, c0=c0: e.tensor_copy(
                        NCB[:, c0:c0 + 8], PD[:, 0:8]),
                        reads=[bPD], writes=[bNCB])
                if DEBUG:
                    kb.emit("sp", lambda e: e.dma_start(
                        out=dbg["d_ncb"], in_=NCB[:, :]), reads=[bNCB], dma=True)
                    kb.emit("sp", lambda e: e.dma_start(
                        out=dbg["d_knq"], in_=KNQ[:, :]), reads=[bKNQ], dma=True)
                    kb.emit("sp", lambda e: e.dma_start(
                        out=dbg["d_kn8"], in_=KN8[:, :]), reads=[bKN8], dma=True)
                    kb.emit("gp", lambda e: e.dma_start(
                        out=dbg["d_qfm"], in_=QFM[:, 0:T]),
                        reads=[bQFM[0]], dma=True)
                    kb.emit("gp", lambda e: e.dma_start(
                        out=dbg["d_agk"], in_=agko_t.ap()[0:128, :]),
                        reads=[bAGKo], dma=True)
                    kb.emit("gp", lambda e: e.dma_start(
                        out=dbg["d_agv"], in_=agvo_t.ap()[0:128, :]),
                        reads=[bAGVo], dma=True)

                kb.emit("dve", lambda e: e.memset(VBUF[:, :], 1.0),
                        writes=[bVBUF[0], bVBUF[1]])

                def ctx_ap(q):
                    if q < 3:
                        return PC[:, q * 129:(q + 1) * 129]
                    if q < 6:
                        return PC[:, 512 + (q - 3) * 129: 512 + (q - 2) * 129]
                    return PD[:, (q - 6) * 129:(q - 5) * 129]
                ctx_wb = [bPC[0], bPC[1], bPD]

                pending_ctx = [None]

                def flush_ctx():
                    if pending_ctx[0] is not None:
                        cgf, rds = pending_ctx[0]
                        pending_ctx[0] = None
                        kb.emit("pe", cgf, reads=rds, writes=ctx_wb)

                for h in range(NH):
                    if h % 8 == 0:
                        ncb_half(h // 8)
                    kbase = (h // 8) * NCORES * 1024 + (h % 8) * 128
                    vbase = (h // 8) * NCORES * T
                    vcol = (h % 8) * 128
                    for blk in range(8):
                        pb = blk % 2
                        kb.emit("sp", lambda e, blk=blk, pb=pb, kbase=kbase:
                                e.dma_start(
                            out=KBUF[:, pb * T:(pb + 1) * T],
                            in_=agko_t.ap()[kbase + blk * 1024:
                                            kbase + blk * 1024 + 128, :]),
                            reads=[bAGKo[h // 8]], writes=[bKBUF[pb]], dma=True)
                        kb.emit("sp", lambda e, blk=blk, pb=pb, vbase=vbase,
                                vcol=vcol: e.dma_start(
                            out=VBUF[:, pb * 1040: pb * 1040 + 1040]
                            .rearrange("p (kt r) -> p kt r", kt=8)[:, :, 0:128],
                            in_=agvo_t.ap()[vbase + blk * T:
                                            vbase + (blk + 1) * T,
                                            vcol:vcol + 128]
                            .rearrange("(kt p) d -> p kt d", p=128)),
                            reads=[bAGVo[h // 8]], writes=[bVBUF[pb]], dma=True)
                        for kt in range(8):
                            nm = "PA" if kt % 2 == 0 else "PB"
                            pten, pbufs = halves[nm]

                            def stg(e, h=h, pb=pb, kt=kt, pten=pten):
                                inst = None
                                for qch in range(2):
                                    inst = e.matmul(
                                        pten[:, qch * 512:(qch + 1) * 512],
                                        KBUF[:, pb * T + kt * 128:
                                             pb * T + (kt + 1) * 128],
                                        QFM[:, h * T + qch * 512:
                                            h * T + (qch + 1) * 512],
                                        start=True, stop=True)
                                return inst
                            kb.emit("pe", stg, reads=[bKBUF[pb], bQFM[h]],
                                    writes=[pbufs[0], pbufs[1]])
                            pcol = (pb * 8 + kt) * 1024
                            kb.emit("act", lambda e, h=h, pcol=pcol, pten=pten:
                                    e.activation(
                                FMB[:, pcol:pcol + 1024], pten[:, 0:1024],
                                AF.Exp, bias=NCB[:, h:h + 1], scale=1.0),
                                reads=[pbufs[0], pbufs[1], bNCB],
                                writes=[bFMB[pcol // 1024]])
                            flush_ctx()
                            first = (blk == 0 and kt == 0)
                            last = (blk == 7 and kt == 7)

                            def cg(e, pb=pb, kt=kt, pcol=pcol, first=first, last=last):
                                inst = None
                                for q in range(8):
                                    st = first and (q in (0, 3, 6))
                                    inst = e.matmul(
                                        ctx_ap(q),
                                        FMB[:, pcol + q * 128: pcol + (q + 1) * 128],
                                        VBUF[:, pb * 1040 + kt * 130:
                                             pb * 1040 + kt * 130 + 129],
                                        start=st, stop=last)
                                return inst
                            pending_ctx[0] = (cg, [bFMB[pcol // 1024], bVBUF[pb]])
                    flush_ctx()
                    for q in range(8):
                        kb.emit("dve", lambda e, q=q: e.reciprocal(
                            SMALL[:, 32 + q:33 + q], ctx_ap(q)[:, 128:129]),
                            reads=ctx_wb, writes=[bSMALL])
                    for q in range(8):
                        kb.emit("dve", lambda e, h=h, q=q: e.tensor_scalar(
                            FMA[:, q * H + h * 128: q * H + (h + 1) * 128],
                            ctx_ap(q)[:, 0:128], SMALL[:, 32 + q:33 + q], None,
                            ALU.mult),
                            reads=ctx_wb + [bSMALL],
                            writes=[bFMA[(q * H + h * 128) // T]])

                if DEBUG:
                    kb.emit("gp", lambda e: e.dma_start(
                        out=dbg["d_ctm"], in_=FMA[:, 0:H]),
                        reads=[bFMA[0], bFMA[1]], dma=True)
                    kb.emit("gp", lambda e: e.dma_start(
                        out=dbg["d_p"], in_=FMB[:, 0:1024]),
                        reads=[bFMB[0]], dma=True)
                for tt in range(NT):
                    for grp in range(2):
                        for k in range(8):
                            c = grp * 8 + k
                            kb.emit("pe", lambda e, tt=tt, c=c, k=k: e.transpose(
                                PT[:, k * 128:(k + 1) * 128],
                                FMA[:, tt * H + c * 128: tt * H + (c + 1) * 128],
                                IDN[:, :]),
                                reads=[bFMA[(tt * H + c * 128) // T], bIDN],
                                writes=[bPT])
                        for k in range(8):
                            c = grp * 8 + k
                            dcol = c * T + tt * 128
                            kb.emit("dve", lambda e, k=k, dcol=dcol: e.tensor_copy(
                                QFM[:, dcol:dcol + 128], PT[:, k * 128:(k + 1) * 128]),
                                reads=[bPT], writes=[bQFM[dcol // T]])
                bSTRAc = [Buf(f"STRA{c}") for c in range(NC)]
                for och in range(4):
                    for c in range(NC):
                        kb.emit("gp", lambda e, och=och, c=c: e.dma_start(
                            out=STRA[:, c * 512:(c + 1) * 512],
                            in_=wp4[och * 16 + c, :, :]),
                            writes=[bSTRAc[c]], dma=True, dsem="straS")
                    if DEBUG and och == 0:
                        kb.emit("gp", lambda e: e.dma_start(
                            out=dbg["d_stra"], in_=STRA[:, :]),
                            reads=list(bSTRAc), dma=True)
                        kb.emit("gp", lambda e: e.dma_start(
                            out=dbg["d_bpb"], in_=BPB[:, :]),
                            reads=[bBPB], dma=True)
                        kb.emit("gp", lambda e: e.dma_start(
                            out=dbg["d_cfm"], in_=QFM[:, 0:T]),
                            reads=[bQFM[0]], dma=True)
                    for ttblk in range(2):
                        accs = [ph("PA", 0), ph("PA", 1), ph("PB", 0), ph("PB", 1)]
                        for c in range(NC):
                            def pj(e, och=och, ttblk=ttblk, c=c, accs=accs):
                                inst = None
                                for q in range(4):
                                    tt = ttblk * 4 + q
                                    inst = e.matmul(
                                        accs[q][0],
                                        QFM[:, c * T + tt * 128: c * T + (tt + 1) * 128],
                                        STRA[:, c * 512:(c + 1) * 512],
                                        start=(c == 0), stop=(c == NC - 1))
                                return inst
                            kb.emit("pe", pj, reads=[bSTRAc[c], bQFM[c]],
                                    writes=[a[1] for a in accs])
                        for q in range(4):
                            tt = ttblk * 4 + q
                            kb.emit("dve", lambda e, q=q, och=och, accs=accs:
                                    e.tensor_tensor(
                                ASCR[:, 0:512], accs[q][0],
                                BPB[:, och * 512:(och + 1) * 512], ALU.add),
                                reads=[accs[q][1], bBPB], writes=[bASCR])
                            kb.emit("dve", lambda e, tt=tt, och=och: e.tensor_tensor(
                                X[:, tt * H + och * 512: tt * H + (och + 1) * 512],
                                X[:, tt * H + och * 512: tt * H + (och + 1) * 512],
                                ASCR[:, 0:512], ALU.add),
                                reads=[bASCR, bX[tt]], writes=[bX[tt]])
                scope_barrier(a_scope + bSTRAc
                              + [b for s in bQSL for b in s])

        ffn_call(0, first=True)
        if phase_limit >= 2:
            attn_call()
        if phase_limit >= 3:
            ffn_call(1)

        for tt in range(NT):
            kb.emit("sp", lambda e, tt=tt: e.dma_start(
                out=out[tt * 128:(tt + 1) * 128, :],
                in_=X[:, tt * H:(tt + 1) * H]), reads=[bX[tt]], dma=True,
                dsem="outs")
        kb.emit("sp", lambda e: e.nop(),
                extra_waits=[("outs", kb.cnt["outs"])])

        block = stack.enter_context(nc.Block())
        kb.replay(block)

    return nc


def kernel(**inputs):
    _ = inputs.get("L0_attn_key")
    _ = inputs.get("L0_attn_value")
    x = np.asarray(inputs["L0_attn_query"], dtype=np.float32).reshape(S, H)
    idx = np.concatenate([np.arange(g * 768 + 640, g * 768 + 768) for g in range(4)])
    qkvw = np.asarray(inputs["L0_ffn_qkv_weight"], dtype=np.float32)
    wvT = np.ascontiguousarray(qkvw[idx].T)
    fprojT = np.ascontiguousarray(
        np.asarray(inputs["L0_ffn_proj_weight"], dtype=np.float32).T)
    wqkvT = np.asarray(inputs["L0_attn_Wqkv"], dtype=np.float32).T
    w3d = np.ascontiguousarray(
        wqkvT.reshape(16, 128, 16, 384).transpose(2, 1, 0, 3)).reshape(
            16, 128, 6144)
    wprojT = np.asarray(inputs["L0_attn_Wproj"], dtype=np.float32).T
    wp4 = np.ascontiguousarray(
        wprojT.reshape(16, 128, 4, 512).transpose(2, 0, 1, 3)).reshape(
            64, 128, 512)
    bprojrow = np.ascontiguousarray(
        np.asarray(inputs["L0_attn_bproj"], dtype=np.float32).reshape(1, H))
    routerw = np.ascontiguousarray(
        np.asarray(inputs["L0_ffn_router_weight"], dtype=np.float32))
    g1c = np.ascontiguousarray(
        np.asarray(inputs["L0_ffn_ln1_weight"], dtype=np.float32).reshape(NC, 128).T)
    b1c = np.ascontiguousarray(
        np.asarray(inputs["L0_ffn_ln1_bias"], dtype=np.float32).reshape(NC, 128).T)
    g2c = np.ascontiguousarray(
        np.asarray(inputs["L0_ffn_ln2_weight"], dtype=np.float32).reshape(NC, 128).T)
    b2c = np.ascontiguousarray(
        np.asarray(inputs["L0_ffn_ln2_bias"], dtype=np.float32).reshape(NC, 128).T)
    ident = np.eye(128, dtype=ml_dtypes.bfloat16)
    moe_w1 = np.asarray(inputs["L0_ffn_moe_w1"], dtype=np.float32)
    moe_w2 = np.asarray(inputs["L0_ffn_moe_w2"], dtype=np.float32)
    w1d = np.ascontiguousarray(
        moe_w1.reshape(8, 16, 128, 16, 256).transpose(0, 3, 2, 1, 4)).reshape(
            128, 128, 4096)
    w2d = np.ascontiguousarray(
        moe_w2.reshape(8, 32, 128, 4, 512).transpose(0, 3, 1, 2, 4)).reshape(
            1024, 128, 512)

    nc = build_program(PHASE_LIMIT)
    in_maps = []
    for i in range(NCORES):
        in_maps.append({
            "x": np.ascontiguousarray(x[i * T:(i + 1) * T]),
            "w3d": w3d, "wp4": wp4, "bprojrow": bprojrow,
            "wvT": wvT, "fprojT": fprojT, "routerw": routerw,
            "g1c": g1c, "b1c": b1c, "g2c": g2c, "b2c": b2c,
            "ident": ident, "w1d": w1d, "w2d": w2d,
        })
    res = run_bass_kernel_spmd(nc, in_maps, core_ids=list(range(NCORES)))
    shards = [np.asarray(res.results[i]["out"], dtype=np.float32)
              for i in range(NCORES)]
    return np.concatenate(shards, axis=0).reshape(1, S, H)
